# version 37
# baseline (speedup 1.0000x reference)
import numpy as np
import concourse.bass as bass
import concourse.mybir as mybir
from concourse.bass_utils import run_bass_kernel_spmd

F32 = mybir.dt.float32
BF16 = mybir.dt.bfloat16
AF = mybir.ActivationFunctionType
ALU = mybir.AluOpType
AX = mybir.AxisListType

NB = 4
T = 2048
NT = 16
C = 1024
NCH = 8
R = 64
EPS = 1e-5


class Ev:
    __slots__ = ("sem", "val", "eng")

    def __init__(self, sem, val, eng):
        self.sem, self.val, self.eng = sem, val, eng


class Sched:
    def __init__(self):
        self.ops = []
        self.cnt = {}

    def op(self, eng, fn, waits=(), inc=None, amt=1):
        w = tuple(e for e in waits if e is not None)
        ev = None
        if inc is not None:
            self.cnt[inc] = self.cnt.get(inc, 0) + amt
            ev = Ev(inc, self.cnt[inc], eng if amt == 1 else None)
        self.ops.append((eng, w, fn, inc, amt))
        return ev


def build_program():
    nc = bass.Bass("TRN2", target_bir_lowering=False)

    x_in = nc.dram_tensor("x_in", [NB * T, C], F32, kind="ExternalInput").ap()
    wv_t = nc.dram_tensor("wv_t", [C, C], F32, kind="ExternalInput").ap()
    wo_t = nc.dram_tensor("wo_t", [C, C], F32, kind="ExternalInput").ap()
    wb_t = nc.dram_tensor("wb_t", [C, R], F32, kind="ExternalInput").ap()
    w_t = nc.dram_tensor("w_t", [C, C], F32, kind="ExternalInput").ap()
    bv_h = nc.dram_tensor("bv", [1, C], F32, kind="ExternalInput").ap()
    bo_h = nc.dram_tensor("bo", [1, C], F32, kind="ExternalInput").ap()
    id_h = nc.dram_tensor("ident", [128, 128], F32, kind="ExternalInput").ap()
    tri_h = nc.dram_tensor("triu", [128, 128], F32, kind="ExternalInput").ap()
    y_out = nc.dram_tensor("y_out", [NB * T, C], F32, kind="ExternalOutput").ap()
    v_scr = nc.dram_tensor("v_scr", [T, 2 * C], BF16, kind="Internal").ap()
    wvl_scr = nc.dram_tensor("wvl_scr", [128, 2 * NCH * C], BF16, kind="Internal").ap()
    wol_scr = nc.dram_tensor("wol_scr", [128, 2 * NCH * C], BF16, kind="Internal").ap()

    S = Sched()

    from contextlib import ExitStack
    with ExitStack() as _st:
        x_sb = _st.enter_context(nc.sbuf_tensor("x_sb", [128, NT * C], F32))
        swt = _st.enter_context(nc.sbuf_tensor("swt", [128, NCH * C], BF16))
        warena = _st.enter_context(nc.sbuf_tensor("warena", [128, 2 * NCH * C], BF16))
        wbth = _st.enter_context(nc.sbuf_tensor("wbth", [128, NCH * R], BF16))
        wbtl = _st.enter_context(nc.sbuf_tensor("wbtl", [128, NCH * R], BF16))
        bvb = _st.enter_context(nc.sbuf_tensor("bvb", [128, C], F32))
        bob = _st.enter_context(nc.sbuf_tensor("bob", [128, C], F32))
        ident = _st.enter_context(nc.sbuf_tensor("ident_sb", [128, 128], F32))
        triu = _st.enter_context(nc.sbuf_tensor("triu_sb", [128, 128], F32))
        bvrow = _st.enter_context(nc.sbuf_tensor("bvrow", [1, C], F32))
        borow = _st.enter_context(nc.sbuf_tensor("borow", [1, C], F32))
        onec = _st.enter_context(nc.sbuf_tensor("onec", [128, 1], F32))
        onecb = _st.enter_context(nc.sbuf_tensor("onecb", [128, 1], BF16))
        oner = _st.enter_context(nc.sbuf_tensor("oner", [1, 128], F32))
        sums = _st.enter_context(nc.sbuf_tensor("sums", [128, 2 * NT], F32))
        sqs = _st.enter_context(nc.sbuf_tensor("sqs", [128, 2 * NT], F32))
        mvec = _st.enter_context(nc.sbuf_tensor("mvec", [128, 2 * NT], F32))
        msq = _st.enter_context(nc.sbuf_tensor("msq", [128, 2 * NT], F32))
        varv = _st.enter_context(nc.sbuf_tensor("varv", [128, 2 * NT], F32))
        sqv = _st.enter_context(nc.sbuf_tensor("sqv", [128, 2 * NT], F32))
        rstd = _st.enter_context(nc.sbuf_tensor("rstd", [128, 2 * NT], F32))
        dump = _st.enter_context(nc.sbuf_tensor("dump", [128, C], F32))
        xh32 = _st.enter_context(nc.sbuf_tensor("xh32", [128, 2 * C], F32))
        xhi = _st.enter_context(nc.sbuf_tensor("xhi", [128, 2 * C], BF16))
        xlo = _st.enter_context(nc.sbuf_tensor("xlo", [128, 2 * C], BF16))
        lath = _st.enter_context(nc.sbuf_tensor("lath", [64, T], BF16))
        latl = _st.enter_context(nc.sbuf_tensor("latl", [64, T], BF16))
        mx4 = _st.enter_context(nc.sbuf_tensor("mx4", [1, 4], F32))
        mxrow = _st.enter_context(nc.sbuf_tensor("mxrow", [1, 1], F32))
        cbias = _st.enter_context(nc.sbuf_tensor("cbias", [128, 1], F32))
        e_sb = _st.enter_context(nc.sbuf_tensor("e_sb", [128, 3 * 128], F32))
        ehb = _st.enter_context(nc.sbuf_tensor("ehb", [128, 3 * 128], BF16))
        elb = _st.enter_context(nc.sbuf_tensor("elb", [128, 3 * 128], BF16))
        vwl = _st.enter_context(nc.sbuf_tensor("vwl", [128, 2 * 2 * C], BF16))
        vr = _st.enter_context(nc.sbuf_tensor("vr", [128, 4 * 2 * C], BF16))
        y32 = _st.enter_context(nc.sbuf_tensor("y32", [128, C], F32))
        yfml = _st.enter_context(nc.sbuf_tensor("yfml", [128, 2 * C], BF16))
        rz = _st.enter_context(nc.sbuf_tensor("rz", [128, 2], F32))
        xsp = _st.enter_context(nc.sbuf_tensor("xsp", [128, C], F32))
        tmpo = _st.enter_context(nc.sbuf_tensor("tmpo", [128, 2 * C], F32))
        psB = _st.enter_context(nc.psum_tensor("psB", [128, 512], F32))
        psC = _st.enter_context(nc.psum_tensor("psC", [128, 512], F32))
        psD = _st.enter_context(nc.psum_tensor("psD", [128, 1024], F32))
        psD2 = _st.enter_context(nc.psum_tensor("psD2", [128, 1024], F32))
        psE = _st.enter_context(nc.psum_tensor("psE", [128, 512], F32))
        psF = _st.enter_context(nc.psum_tensor("psF", [128, 512], F32))
        bank = lambda i: psD if i % 2 == 0 else psD2
        sem = {}
        for nm in (["T", "V", "S", "G", "dWG", "dWV", "dWO"]
                   + [f"dIN{i}" for i in range(4)]
                   + [f"dVO{i}" for i in range(2)]
                   + [f"dXI{i}" for i in range(4)]
                   + [f"dXO{i}" for i in range(4)]
                   + [f"dVG{i}" for i in range(4)]
                   + [f"dSC{i}" for i in range(2)]):
            sem[nm] = _st.enter_context(nc.semaphore("sem_" + nm))
        block = _st.enter_context(nc.Block())

        for dst, srch in ((triu, tri_h), (bvrow, bv_h),
                          (borow, bo_h)):
            S.op("gpsimd",
                 lambda e, d=dst, s=srch: e.dma_start(d[:, :], s[:, :]),
                 inc="dWG", amt=16)
        ev_consts_all = S.op("gpsimd",
                             lambda e: e.dma_start(ident[:, :], id_h[:, :]),
                             inc="dWG", amt=16)
        ev_ones = S.op("vector", lambda e: e.memset(onec[:, :], 1.0), inc="V")
        ev_onesb = S.op("vector", lambda e: e.memset(onecb[:, :], 1.0), inc="V")
        ev_oner = S.op("vector", lambda e: e.memset(oner[:, :], 1.0), inc="V")

        ev_bb = {}
        for bi, (row, tile) in enumerate(((bvrow, bvb), (borow, bob))):
            for h, ps in ((0, psB), (1, psC)):
                evm = S.op("tensor",
                           lambda e, r=row, h=h, ps=ps: e.matmul(
                               ps[:, 0:512], oner[0:1, :], r[0:1, h * 512:(h + 1) * 512],
                               start=True, stop=True),
                           waits=[ev_oner, ev_consts_all] + ([ev_bb.get((bi - 1, h))] if bi else []),
                           inc="T")
                ev_bb[(bi, h)] = S.op("vector",
                                      lambda e, t=tile, h=h, ps=ps: e.tensor_copy(
                                          t[:, h * 512:(h + 1) * 512], ps[:, 0:512]),
                                      waits=[evm], inc="V")

        LO = NCH * C
        stg_bufs = [(tmpo, 0), (tmpo, C), (dump, 0), (y32, 0)]
        stg_free = [None, None, None, None]
        last_dIN = {}

        def stage4(i, src_slice):
            buf, off = stg_bufs[i]
            eng, sm = (("sync", f"dIN{i}") if i < 2
                       else ("scalar", f"dSC{i - 2}"))
            evd = S.op(eng,
                       lambda e, s=src_slice, buf=buf, off=off: e.dma_start(
                           buf[:, off:off + C], s),
                       waits=[stg_free[i]], inc=sm, amt=16)
            if i < 2:
                last_dIN[i] = evd
            return evd

        ev_sign = [None] * NCH
        dma_ev = {}
        for ic in range(4):
            dma_ev[ic] = stage4(ic, w_t[ic * 128:(ic + 1) * 128, :])
        for ic in range(NCH):
            evd = dma_ev[ic]
            buf, off = stg_bufs[ic % 4]
            ev_sign[ic] = S.op("scalar",
                               lambda e, ic=ic, buf=buf, off=off: e.activation(
                                   swt[:, ic * C:(ic + 1) * C],
                                   buf[:, off:off + C], AF.Sign),
                               waits=[evd], inc="S")
            stg_free[ic % 4] = ev_sign[ic]
            if ic + 4 < NCH:
                dma_ev[ic + 4] = stage4(
                    ic, w_t[(ic + 4) * 128:(ic + 5) * 128, :])

        wbev = {}

        def emit_wb():
            evd_wb = S.op("sync",
                          lambda e: e.dma_start(
                              tmpo[:, 0:512].rearrange("p (ic r) -> p ic r",
                                                       ic=NCH),
                              wb_t.rearrange("(ic p) r -> p ic r", p=128)),
                          waits=[stg_free[0]], inc="dIN0", amt=16)
            last_dIN[0] = evd_wb
            wbev["h"] = S.op("scalar",
                             lambda e: e.activation(wbth[:, :], tmpo[:, 0:512],
                                                    AF.Copy),
                             waits=[evd_wb], inc="S")
            wbev["l"] = S.op("vector",
                             lambda e: e.tensor_sub(wbtl[:, :], tmpo[:, 0:512],
                                                    wbth[:, :]),
                             waits=[wbev["h"]], inc="V")
            stg_free[0] = wbev["l"]

        init_free = [None, None]
        ev_spill = {}
        wv_ready = [None]

        def emit_wchunk(j):
            if j == 0:
                init_free[0], init_free[1] = stg_free[0], stg_free[1]
            w_src, ic = (wo_t, j) if j < NCH else (wv_t, j - NCH)
            slot = j % 2
            evd = S.op("sync",
                       lambda e, s=w_src[ic * 128:(ic + 1) * 128, :],
                       slot=slot: e.dma_start(
                           tmpo[:, slot * C:(slot + 1) * C], s),
                       waits=[init_free[slot]], inc=f"dIN{slot}", amt=16)
            last_dIN[slot] = evd
            evh = S.op("scalar",
                       lambda e, ic=ic, slot=slot: e.activation(
                           warena[:, ic * C:(ic + 1) * C],
                           tmpo[:, slot * C:(slot + 1) * C], AF.Copy),
                       waits=[evd, ev_spill.get("wo")], inc="S")
            evl = S.op("vector",
                       lambda e, ic=ic, slot=slot: e.tensor_sub(
                           warena[:, LO + ic * C:LO + (ic + 1) * C],
                           tmpo[:, slot * C:(slot + 1) * C],
                           warena[:, ic * C:(ic + 1) * C]),
                       waits=[evh], inc="V")
            init_free[slot] = evl
            if j == NCH - 1:
                ev_spill["wo"] = S.op("sync",
                                      lambda e: e.dma_start(wol_scr[:, :],
                                                            warena[:, :]),
                                      waits=[evl], inc="dWV", amt=16)
            elif j == 2 * NCH - 1:
                wv_ready[0] = evl
                ev_spill["wv"] = S.op("sync",
                                      lambda e: e.dma_start(wvl_scr[:, :],
                                                            warena[:, :]),
                                      waits=[evl], inc="dWV", amt=16)

        ev_xout = {}
        ev_wvt = [None, None]
        ev_wot = None
        ev_outp_last = None
        carry_ffn2_hilo = {0: (ev_consts_all,), 1: (ev_consts_all,)}
        carry_accum = {(0, 0): None, (1, 0): None,
                       (0, 1): ev_bb[(1, 0)], (1, 1): ev_bb[(1, 1)]}
        carry_evrz15 = None
        carry_xin = {}
        carry_flush = None
        carry_pre_eva = {}
        x_hbm = x_in.rearrange("(k t p) c -> k p t c", p=128, t=4)
        y_hbm = y_out.rearrange("(k t p) c -> k p t c", p=128, t=4)

        def sb4(k):
            return x_sb[:, k * 4 * C:(k + 1) * 4 * C].rearrange(
                "p (t c) -> p t c", t=4)

        evsq_rd_done = [stg_free[2]]
        tr_ev = [None, None]

        def emit_stat_tile(po, ti, gates, src=None):
            sv = src if src is not None else x_sb[:, ti * C:(ti + 1) * C]
            evsum = S.op("vector",
                         lambda e, po=po, ti=ti, sv=sv: e.reduce_sum(
                             sums[:, po + ti:po + ti + 1], sv, axis=AX.X),
                         waits=gates, inc="V")
            evsq = S.op("scalar",
                        lambda e, ti=ti, sv=sv: e.activation(
                            dump[:, :], sv, AF.Square),
                        waits=list(gates) + [evsq_rd_done[0]], inc="S")
            evsq2 = S.op("vector",
                         lambda e, po=po, ti=ti: e.reduce_sum(
                             sqs[:, po + ti:po + ti + 1], dump[:, :], axis=AX.X),
                         waits=[evsq], inc="V")
            evsq_rd_done[0] = evsq2
            return [evsum, evsq2]

        def emit_finish(po, g, stat_evs):
            c0 = po + 4 * g
            sl = slice(c0, c0 + 4)
            evm0 = S.op("vector",
                        lambda e, sl=sl: e.tensor_scalar_mul(
                            mvec[:, sl], sums[:, sl], 1.0 / C),
                        waits=stat_evs, inc="V")
            evm1 = S.op("vector",
                        lambda e, sl=sl: e.tensor_mul(msq[:, sl], mvec[:, sl],
                                                      mvec[:, sl]),
                        waits=[evm0], inc="V")
            evv = S.op("vector",
                       lambda e, sl=sl: e.tensor_scalar(
                           varv[:, sl], sqs[:, sl], 1.0 / C, None, ALU.mult),
                       waits=stat_evs, inc="V")
            evv2 = S.op("vector",
                        lambda e, sl=sl: e.tensor_sub(varv[:, sl], varv[:, sl],
                                                      msq[:, sl]),
                        waits=[evm1, evv], inc="V")
            evv3 = S.op("vector",
                        lambda e, sl=sl: e.tensor_scalar_add(varv[:, sl],
                                                             varv[:, sl], EPS),
                        waits=[evv2], inc="V")
            evsq = S.op("scalar",
                        lambda e, sl=sl: e.activation(sqv[:, sl], varv[:, sl],
                                                      AF.Sqrt),
                        waits=[evv3], inc="S")
            evr = S.op("vector",
                       lambda e, sl=sl: e.reciprocal(rstd[:, sl], sqv[:, sl]),
                       waits=[evsq], inc="V")
            return evr

        ev_xin0 = {}
        for ti in range(NT):
            g, t = ti // 4, ti % 4
            ev_xin0[ti] = S.op("gpsimd",
                               lambda e, g=g, t=t: e.dma_start(
                                   x_sb[:, (4 * g + t) * C:
                                        (4 * g + t + 1) * C],
                                   x_hbm[g, :, t, :]),
                               waits=[ev_xin0.get(ti - 4)],
                               inc=f"dXI{ti % 4}", amt=16)
        finish_ffn1_b0 = {}
        b0_sevs = {}

        def emit_b0_tile(tj):
            g = tj // 4
            sevs = b0_sevs.setdefault(g, [])
            sevs += emit_stat_tile(0, tj, [ev_xin0[tj]])
            if tj % 4 == 3:
                finish_ffn1_b0[g] = emit_finish(0, g, sevs)

        for tj in range(8):
            emit_b0_tile(tj)
        carry_xin = dict(ev_xin0)

        for b in range(NB):
            finish_ffn1 = finish_ffn1_b0 if b == 0 else carry_finish_ffn1

            def ln_apply(ti, po, gate):
                sl = (ti % 2) * C
                war = [tr_ev[(ti - 2) % 2]] if tr_ev[(ti - 2) % 2] else []
                return S.op("vector",
                            lambda e, ti=ti, sl=sl, po=po: e.tensor_scalar(
                                xh32[:, sl:sl + C], x_sb[:, ti * C:(ti + 1) * C],
                                mvec[:, po + ti:po + ti + 1],
                                rstd[:, po + ti:po + ti + 1],
                                ALU.subtract, ALU.mult),
                            waits=[gate] + war,
                            inc="V")

            def ln_transpose(ti, eva, tr_free_evs):
                sl = (ti % 2) * C

                def fn(e, sl=sl, bk=bank(ti)):
                    last = None
                    for ic in range(NCH):
                        last = e.transpose(bk[:, ic * 128:(ic + 1) * 128],
                                           xh32[:, sl + ic * 128:sl + (ic + 1) * 128],
                                           ident[:, :])
                    return last
                evt = S.op("tensor", fn, waits=[eva] + tr_free_evs, inc="T")
                tr_ev[ti % 2] = evt
                return evt

            def run_ffn(sub, po, finish_evs, tr_carry, accum_carry,
                        tile_hook=None, inject_at2=None, pre_eva=None):
                evac_ev = dict(accum_carry)
                hilo_ev = {}
                mm_ev = {}
                trs = {}
                eva_d = {}

                def emit_eva(ti):
                    if ti >= NT:
                        return
                    if pre_eva and ti in pre_eva:
                        eva_d[ti] = pre_eva[ti]
                        return
                    eva_d[ti] = ln_apply(ti, po, finish_evs[ti // 4])

                def emit_tr(ti):
                    if ti >= NT:
                        return
                    free = ([hilo_ev[(ti - 2, "hi")], hilo_ev[(ti - 2, "lo")]]
                            if ti >= 2 else list(tr_carry.get(ti % 2, ())))
                    trs[ti] = ln_transpose(ti, eva_d[ti],
                                           [f for f in free if f])
                def emit_hilo(ti):
                    if ti >= NT:
                        return
                    evt = trs[ti]
                    sl = (ti % 2) * C
                    bk = bank(ti)
                    war = [mm_ev[(ti - 2, 1)]] if (ti - 2, 1) in mm_ev else []
                    ehi = S.op("scalar",
                               lambda e, sl=sl, bk=bk: e.activation(
                                   xhi[:, sl:sl + C], bk[:, :], AF.Copy),
                               waits=[evt] + war, inc="S")
                    elo = S.op("vector",
                               lambda e, sl=sl, bk=bk: e.tensor_sub(
                                   xlo[:, sl:sl + C], bk[:, :],
                                   xhi[:, sl:sl + C]),
                               waits=[evt, ehi] + war, inc="V")
                    hilo_ev[(ti, "hi")] = ehi
                    hilo_ev[(ti, "lo")] = elo
                emit_eva(0)
                emit_eva(1)
                emit_tr(0)
                emit_eva(2)
                emit_tr(1)
                emit_hilo(0)
                for ti in range(NT):
                    if ti == 6 and inject_at2 is not None:
                        inject_at2()
                    emit_eva(ti + 3)
                    emit_tr(ti + 2)
                    emit_hilo(ti + 1)
                    ehi = hilo_ev[(ti, "hi")]
                    elo = hilo_ev[(ti, "lo")]
                    sl = (ti % 2) * C
                    par = ti % 2
                    for h, ps in ((0, psE if par == 0 else psB),
                                  (1, psF if par == 0 else psC)):
                        def fn(e, sl=sl, h=h, ps=ps):
                            last = None
                            for j, src in enumerate((xhi, xlo)):
                                for ic in range(NCH):
                                    last = e.matmul(
                                        ps[:, 0:512],
                                        src[:, sl + ic * 128:sl + (ic + 1) * 128],
                                        swt[:, ic * C + h * 512:ic * C + (h + 1) * 512],
                                        start=(j == 0 and ic == 0),
                                        stop=(j == 1 and ic == NCH - 1))
                            return last
                        evm = S.op("tensor", fn,
                                   waits=[ehi, elo, evac_ev[(h, par)],
                                          ev_sign[7]], inc="T")
                        mm_ev[(ti, h)] = evm
                        evac_ev[(h, par)] = S.op(
                            "vector",
                            lambda e, ti=ti, h=h, ps=ps:
                            e.scalar_tensor_tensor(
                                x_sb[:, ti * C + h * 512:ti * C + (h + 1) * 512],
                                ps[:, 0:512], 0.5,
                                x_sb[:, ti * C + h * 512:ti * C + (h + 1) * 512],
                                ALU.mult, ALU.add),
                            waits=[evm], inc="V")
                        ffn_evac[(sub, ti, h)] = evac_ev[(h, par)]
                    if tile_hook is not None:
                        tile_hook(ti)
                return hilo_ev

            ffn_evac = {}

            po_f1 = (b % 2) * NT
            po_at = ((b + 1) % 2) * NT
            po_f2 = (b % 2) * NT
            finish_attn = {}

            at_stats = {}

            def attn_hook(ti):
                if b == 0:
                    if ti == 0:
                        emit_wb()
                    if ti < 8:
                        emit_b0_tile(8 + ti)
                sevs = at_stats.setdefault(ti // 4, [])
                sevs += emit_stat_tile(
                    po_at, ti, [ffn_evac[(0, ti, 0)], ffn_evac[(0, ti, 1)]])
                if ti % 4 == 3:
                    finish_attn[ti // 4] = emit_finish(po_at, ti // 4, sevs)
                if b == 0:
                    emit_wchunk(ti)

            hilo1 = run_ffn(0, po_f1, finish_ffn1,
                            tr_carry=carry_ffn2_hilo, accum_carry=carry_accum,
                            tile_hook=attn_hook, inject_at2=carry_flush,
                            pre_eva=carry_pre_eva)

            if b == 0:
                ev_wvt = [None, wv_ready[0]]
            else:
                ev_wvt = []
                for k in range(2):
                    ev_wvt.append(S.op("sync",
                                       lambda e, k=k: e.dma_start(
                                           warena[:, k * LO:(k + 1) * LO],
                                           wvl_scr[:, k * LO:(k + 1) * LO]),
                                       waits=[ev_outp_last, ev_spill["wv"]],
                                       inc="dWV", amt=16))

            lat_ev = None
            xfm_ev = {}
            vmm_done = {}
            vh_ev = {}
            vl_ev = {}
            vout_ev = {}
            atr = {}

            aeva_d = {}

            def emit_aeva(ti):
                if ti >= NT:
                    return
                aeva_d[ti] = ln_apply(ti, po_at, finish_attn[ti // 4])

            def emit_atr(ti):
                if ti >= NT:
                    return
                free = ([xfm_ev[ti - 2]] if ti >= 2 else
                        [hilo1.get((14 + ti, "hi")), hilo1.get((14 + ti, "lo"))])
                atr[ti] = ln_transpose(ti, aeva_d[ti], [f for f in free if f])
            evxh_d = {}

            def emit_bx(ti):
                if ti >= NT:
                    return
                evt = atr[ti]
                sl = (ti % 2) * C
                bk = bank(ti)
                free = [vmm_done[ti - 2]] if ti - 2 in vmm_done else []
                evxh = S.op("scalar",
                            lambda e, sl=sl, bk=bk: e.activation(
                                xhi[:, sl:sl + C], bk[:, :], AF.Copy),
                            waits=[evt] + free, inc="S")
                evxl = S.op("vector",
                            lambda e, sl=sl, bk=bk: e.tensor_sub(
                                xlo[:, sl:sl + C], bk[:, :],
                                xhi[:, sl:sl + C]),
                            waits=[evxh], inc="V")
                evxh_d[ti] = evxh
                xfm_ev[ti] = evxl
            emit_aeva(0)
            emit_aeva(1)
            emit_atr(0)
            emit_aeva(2)
            emit_atr(1)
            emit_bx(0)
            for ti in range(NT):
                emit_aeva(ti + 3)
                emit_atr(ti + 2)
                emit_bx(ti + 1)
                evxh = evxh_d[ti]
                evxl = xfm_ev[ti]
                sl = (ti % 2) * C

                def fn_lat(e, sl=sl, ti=ti):
                    last = None
                    for ic in range(NCH):
                        for j, (wl, src) in enumerate(
                                ((wbth, xhi), (wbtl, xhi), (wbth, xlo))):
                            last = e.matmul(
                                psF[0:64, 0:128],
                                wl[:, ic * R:(ic + 1) * R],
                                src[:, sl + ic * 128:sl + (ic + 1) * 128],
                                start=(ic == 0 and j == 0),
                                stop=(ic == NCH - 1 and j == 2))
                    return last
                evlm = S.op("tensor", fn_lat,
                            waits=[evxh, evxl, wbev["h"], wbev["l"],
                                   lat_ev if ti else
                                   ffn_evac[(0, NT - 2, 1)]], inc="T")
                lh_ev = S.op("scalar",
                             lambda e, ti=ti: e.activation(
                                 lath[0:64, ti * 128:(ti + 1) * 128],
                                 psF[0:64, 0:128], AF.Copy),
                             waits=[evlm], inc="S")
                lat_ev = S.op("vector",
                              lambda e, ti=ti: e.tensor_sub(
                                  latl[0:64, ti * 128:(ti + 1) * 128],
                                  psF[0:64, 0:128],
                                  lath[0:64, ti * 128:(ti + 1) * 128]),
                              waits=[lh_ev], inc="V")

                sl3 = (ti % 2) * 2 * C
                for h, ps in ((0, psB), (1, psC)):
                    def fn_v(e, sl=sl, h=h, ps=ps):
                        last = None
                        for ic in range(NCH):
                            for j, (src, woff) in enumerate(
                                    ((xhi, 0), (xhi, LO), (xlo, 0))):
                                last = e.matmul(
                                    ps[:, 0:512],
                                    src[:, sl + ic * 128:sl + (ic + 1) * 128],
                                    warena[:, woff + ic * C + h * 512:
                                           woff + ic * C + (h + 1) * 512],
                                    start=(ic == 0 and j == 0),
                                    stop=(ic == NCH - 1 and j == 2))
                        return last
                    evm = S.op("tensor", fn_v,
                               waits=[evxh, evxl, ev_wvt[1],
                                      vl_ev.get((ti - 1, h)),
                                      ffn_evac[(0, NT - 1, h)]],
                               inc="T")
                    vmm_done[ti] = evm
                    vh_ev[(ti, h)] = S.op(
                        "scalar",
                        lambda e, sl3=sl3, h=h, ps=ps: e.activation(
                            vwl[:, sl3 + h * 512:sl3 + (h + 1) * 512],
                            ps[:, 0:512], AF.Copy),
                        waits=[evm, vout_ev.get(ti - 2)], inc="S")
                    vl_ev[(ti, h)] = S.op(
                        "vector",
                        lambda e, sl3=sl3, h=h, ps=ps: e.tensor_sub(
                            vwl[:, sl3 + C + h * 512:sl3 + C + (h + 1) * 512],
                            ps[:, 0:512],
                            vwl[:, sl3 + h * 512:sl3 + (h + 1) * 512]),
                        waits=[vh_ev[(ti, h)]], inc="V")
                vout_ev[ti] = S.op("scalar",
                                   lambda e, ti=ti, sl3=sl3: e.dma_start(
                                       v_scr[ti * 128:(ti + 1) * 128, :],
                                       vwl[:, sl3:sl3 + 2 * C]),
                                   waits=[vl_ev[(ti, 0)], vl_ev[(ti, 1)]],
                                   inc=f"dVO{ti % 2}", amt=16)

            n2_banks = [psB, psC, psF, psD]
            n2_ev = []
            prev_evac = None
            for j in range(4):
                eva2 = S.op("vector",
                            lambda e, j=j: e.tensor_add(
                                dump[0:64, 512:1024],
                                lath[0:64, j * 512:(j + 1) * 512],
                                latl[0:64, j * 512:(j + 1) * 512]),
                            waits=[lat_ev, prev_evac, evsq_rd_done[0]],
                            inc="V")
                evq = S.op("scalar",
                           lambda e: e.activation(
                               dump[0:64, 512:1024], dump[0:64, 512:1024],
                               AF.Square),
                           waits=[eva2], inc="S")
                wait_bank = []
                if n2_banks[j] is psB:
                    wait_bank = [vl_ev[(NT - 1, 0)]]
                elif n2_banks[j] is psC:
                    wait_bank = [vl_ev[(NT - 1, 1)]]
                elif n2_banks[j] is psD:
                    wait_bank = [xfm_ev[NT - 2]]
                evm = S.op("tensor",
                           lambda e, j=j: e.matmul(
                               n2_banks[j][0:1, 0:512], onec[0:64, 0:1],
                               dump[0:64, 512:1024], start=True, stop=True),
                           waits=[evq, ev_ones] + wait_bank, inc="T")
                prev_evac = S.op("vector",
                                 lambda e, j=j: e.reduce_max(
                                     mx4[0:1, j:j + 1],
                                     n2_banks[j][0:1, 0:512], axis=AX.X),
                                 waits=[evm], inc="V")
                n2_ev.append(prev_evac)
            evmx = S.op("vector",
                        lambda e: e.reduce_max(mxrow[0:1, 0:1], mx4[0:1, :],
                                               axis=AX.X),
                        waits=n2_ev, inc="V")
            evcb_m = S.op("tensor",
                          lambda e: e.matmul(psE[:, 0:1], oner[0:1, :],
                                             mxrow[0:1, 0:1], start=True, stop=True),
                          waits=[evmx, ev_oner], inc="T")
            evcb = S.op("vector",
                        lambda e: e.tensor_scalar_mul(cbias[:, :], psE[:, 0:1],
                                                      -0.125),
                        waits=[evcb_m], inc="V")

            ev_wot = []
            for k in range(2):
                ev_wot.append(S.op("sync",
                                   lambda e, k=k: e.dma_start(
                                       warena[:, k * LO:(k + 1) * LO],
                                       wol_scr[:, k * LO:(k + 1) * LO]),
                                   waits=[vmm_done[NT - 1]] +
                                         ([ev_spill["wv"]] if b == 0 else []),
                                   inc="dWO", amt=16))

            pairs = [(qi, ki) for qi in range(NT) for ki in range(qi + 1)]
            vin_ev = {}
            ving_ev = {}
            emitted = [0]

            def emit_vin(n):
                while emitted[0] < min(n, len(pairs)):
                    idx = emitted[0]
                    qi, ki = pairs[idx]
                    slot = idx % 4
                    wait_slot = ([yz_ev[idx - 4]] if idx >= 4 else
                                 [last_dIN.get(slot)])
                    vin_ev[idx] = S.op(
                        "sync",
                        lambda e, ki=ki, slot=slot: e.dma_start(
                            vr[:, slot * 2 * C:(slot + 1) * 2 * C],
                            v_scr[ki * 128:(ki + 1) * 128, :]),
                        waits=wait_slot + [vout_ev[ki]],
                        inc=f"dIN{slot}", amt=16)
                    emitted[0] += 1

            yz_ev = {}
            exp_ev = {}
            scr = {}
            yfl_ev = {}
            ytr_ev = {}
            evrz_ev = {}
            outp_evac = {0: None, 1: None}
            outp_holder = [None]
            gadd_ev = {}
            pend_tail = []
            finish_ffn2 = {}
            f2stats = {}
            emit_vin(4)

            def emit_f2_stats(ti):
                sevs = f2stats.setdefault(ti // 4, [])
                sevs += emit_stat_tile(po_f2, ti,
                                       [gadd_ev[(ti, 0)], gadd_ev[(ti, 1)]])
                if ti % 4 == 3:
                    finish_ffn2[ti // 4] = emit_finish(po_f2, ti // 4, sevs)

            def emit_score(idx):
                if idx >= len(pairs):
                    return
                qi, ki = pairs[idx]
                slot = idx % 3
                esl = 128 + slot * 128
                es = slot * 128
                def fn_st(e, qi=qi, ki=ki, esl=esl):
                    last = None
                    for j, (kl, ql) in enumerate(
                            ((lath, lath), (lath, latl), (latl, lath))):
                        last = e.matmul(
                            psE[:, esl:esl + 128],
                            kl[0:64, ki * 128:(ki + 1) * 128],
                            ql[0:64, qi * 128:(qi + 1) * 128],
                            start=(j == 0), stop=(j == 2))
                    return last
                evst = S.op("tensor", fn_st,
                            waits=[lat_ev, evcb, exp_ev.get(idx - 1)], inc="T")
                eve = S.op("scalar",
                           lambda e, esl=esl, es=es: e.activation(
                               e_sb[:, es:es + 128], psE[:, esl:esl + 128],
                               AF.Exp, bias=cbias[:, 0:1], scale=0.125),
                           waits=[evst, evcb, yz_ev.get(idx - 3)], inc="S")
                exp_ev[idx] = eve
                gate = eve
                if ki == qi:
                    gate = S.op("vector",
                                lambda e, es=es: e.tensor_mul(
                                    e_sb[:, es:es + 128], e_sb[:, es:es + 128],
                                    triu[:, :]),
                                waits=[eve, ev_consts_all], inc="V")
                evh = S.op("vector",
                           lambda e, es=es: e.tensor_copy(
                               ehb[:, es:es + 128], e_sb[:, es:es + 128]),
                           waits=[gate], inc="V")
                evl = S.op("vector",
                           lambda e, es=es: e.tensor_sub(
                               elb[:, es:es + 128], e_sb[:, es:es + 128],
                               ehb[:, es:es + 128]),
                           waits=[evh], inc="V")
                scr[idx] = (gate, evh, evl)

            def emit_rz(qi):
                idx_d = qi * (qi + 1) // 2 + qi
                zc = qi % 2
                evrz_ev[qi] = S.op("vector",
                                   lambda e, zc=zc: e.reciprocal(
                                       rz[:, zc:zc + 1], psF[:, 0:1]),
                                   waits=[yz_ev[idx_d]], inc="V")

            def emit_tail_a(qi):
                pd = bank(qi)
                zc = qi % 2
                y32_ev = S.op("vector",
                              lambda e, pd=pd, zc=zc: e.scalar_tensor_tensor(
                                  y32[:, :], pd[:, :], rz[:, zc:zc + 1],
                                  bvb[:, :], ALU.mult, ALU.add),
                              waits=[evrz_ev[qi], ev_bb[(0, 0)], ev_bb[(0, 1)],
                                     ytr_ev.get(qi - 1)],
                              inc="V")

                def fn_ytr(e, pd=pd):
                    last = None
                    for cc in range(NCH):
                        last = e.transpose(pd[:, cc * 128:(cc + 1) * 128],
                                           y32[:, cc * 128:(cc + 1) * 128],
                                           ident[:, :])
                    return last
                ytr_ev[qi] = S.op("tensor", fn_ytr, waits=[y32_ev], inc="T")

            def emit_tail_b(qi):
                pd = bank(qi)
                evyt = ytr_ev[qi]
                yfh = S.op("scalar",
                           lambda e, pd=pd: e.activation(yfml[:, 0:C],
                                                         pd[:, :], AF.Copy),
                           waits=[evyt], inc="S")
                yfl = S.op("vector",
                           lambda e, pd=pd: e.tensor_sub(
                               yfml[:, C:2 * C], pd[:, :], yfml[:, 0:C]),
                           waits=[yfh], inc="V")
                yfl_ev[qi] = yfl
                for h, ps in ((0, psB), (1, psC)):
                    def fn_op(e, h=h, ps=ps):
                        last = None
                        for cc in range(NCH):
                            for j, (aoff, woff) in enumerate(
                                    ((0, 0), (0, LO), (C, 0))):
                                last = e.matmul(
                                    ps[:, 0:512],
                                    yfml[:, aoff + cc * 128:
                                         aoff + (cc + 1) * 128],
                                    warena[:, woff + cc * C + h * 512:
                                           woff + cc * C + (h + 1) * 512],
                                    start=(cc == 0 and j == 0),
                                    stop=(cc == NCH - 1 and j == 2))
                        return last
                    wb = [yfh, yfl, ev_wot[1], outp_evac[h]]
                    if qi == 0:
                        wb.append(n2_ev[0 if h == 0 else 1])
                    evm = S.op("tensor", fn_op, waits=wb, inc="T")
                    outp_holder[0] = evm
                    evv = S.op("vector",
                               lambda e, qi=qi, h=h, ps=ps: e.tensor_add(
                                   tmpo[:, (qi % 2) * C + h * 512:
                                        (qi % 2) * C + (h + 1) * 512],
                                   ps[:, 0:512], bob[:, h * 512:(h + 1) * 512]),
                               waits=[evm, ev_bb[(1, h)],
                                      gadd_ev.get((qi - 2, h))] +
                                     ([init_free[qi % 2]]
                                      if (b == 0 and qi < 2) else []),
                               inc="V")
                    outp_evac[h] = evv
                    gadd_ev[(qi, h)] = S.op(
                        "gpsimd",
                        lambda e, qi=qi, h=h: e.tensor_add(
                            x_sb[:, qi * C + h * 512:qi * C + (h + 1) * 512],
                            x_sb[:, qi * C + h * 512:qi * C + (h + 1) * 512],
                            tmpo[:, (qi % 2) * C + h * 512:
                                 (qi % 2) * C + (h + 1) * 512]),
                        waits=[evv], inc="G")
                if qi >= 1:
                    emit_f2_stats(qi - 1)

            emit_score(0)
            emit_score(1)
            pre_eva2 = {}
            for idx, (qi, ki) in enumerate(pairs):
                emit_vin(idx + 4)
                while pend_tail and pend_tail[0][0] <= idx:
                    pend_tail.pop(0)[1]()
                if idx == 104:
                    pre_eva2[0] = ln_apply(0, po_f2, finish_ffn2[0])
                    pre_eva2[1] = ln_apply(1, po_f2, finish_ffn2[0])
                emit_score(idx + 2)
                gate, evh, evl = scr[idx]
                es = (idx % 3) * 128
                vslot = idx % 4
                pd = bank(qi)

                def fn_yz(e, qi=qi, ki=ki, vslot=vslot, es=es, pd=pd):
                    st = ki == 0
                    sp = ki == qi
                    base = vslot * 2 * C
                    for j, (ab, voff) in enumerate(((ehb, 0), (ehb, C), (elb, 0))):
                        e.matmul(pd[:, 0:512], ab[:, es:es + 128],
                                 vr[:, base + voff:base + voff + 512],
                                 start=(st and j == 0), stop=(sp and j == 2))
                        e.matmul(pd[:, 512:1024], ab[:, es:es + 128],
                                 vr[:, base + voff + 512:base + voff + 1024],
                                 start=(st and j == 0), stop=(sp and j == 2))
                    return e.matmul(psF[:, 0:1],
                                    e_sb[:, es:es + 128],
                                    onec[:, 0:1], start=st, stop=sp)
                w = [gate, evh, evl, vin_ev[idx]]
                if ki == 0:
                    if qi >= 1:
                        w.append(evrz_ev[qi - 1])
                    if qi >= 2:
                        w.append(yfl_ev[qi - 2])
                    elif qi < 2:
                        w.append(xfm_ev[NT - 2 + qi])
                yz_ev[idx] = S.op("tensor", fn_yz, waits=w, inc="T")
                if ki == qi:
                    emit_rz(qi)
                    pend_tail.append((idx + 2, lambda qi=qi: emit_tail_a(qi)))
                    pend_tail.append((idx + 3, lambda qi=qi: emit_tail_b(qi)))
            for _, fn in pend_tail:
                fn()
            ev_outp_last = outp_holder[0]
            carry_evrz15 = evrz_ev[NT - 1]

            carry_finish_ffn1 = {}
            po_nx = ((b + 1) % 2) * NT
            xin_tile = {}
            nx_stats = {}
            xsp_cp = [None]
            pre_eva_nx = {}

            def emit_nx_stats(ti):
                sevs = nx_stats.setdefault(ti // 4, [])
                sevs += emit_stat_tile(
                    po_nx, ti, [xin_tile[ti]],
                    src=(xsp[:, :] if ti == NT - 1 else None))
                if ti == NT - 1:
                    sevs.append(xsp_cp[0])
                if ti % 4 == 3:
                    carry_finish_ffn1[ti // 4] = emit_finish(
                        po_nx, ti // 4, sevs)

            def store_hook(ti):
                if ti == 0:
                    emit_f2_stats(NT - 1)
                g, t = ti // 4, ti % 4
                ev_xout[(b, ti)] = S.op(
                    "gpsimd",
                    lambda e, b=b, g=g, t=t, ti=ti: e.dma_start(
                        y_hbm[b * 4 + g, :, t, :],
                        x_sb[:, ti * C:(ti + 1) * C]),
                    waits=[ffn_evac[(2, ti, 0)], ffn_evac[(2, ti, 1)]],
                    inc=f"dXO{ti % 4}", amt=16)
                if b + 1 < NB:
                    if ti == NT - 1:
                        xin_tile[ti] = S.op(
                            "gpsimd",
                            lambda e, b=b: e.dma_start(
                                xsp[:, :], x_hbm[(b + 1) * 4 + 3, :, 3, :]),
                            waits=[xin_tile.get(ti - 4,
                                                carry_xin.get(ti + 12)),
                                   finish_ffn1[3]],
                            inc=f"dXI{ti % 4}", amt=16)
                        xsp_cp[0] = S.op(
                            "gpsimd",
                            lambda e, ti=ti: e.tensor_copy(
                                x_sb[:, ti * C:(ti + 1) * C], xsp[:, :]),
                            waits=[ev_xout[(b, ti)], xin_tile[ti]], inc="G")
                    else:
                        xin_tile[ti] = S.op(
                            "gpsimd",
                            lambda e, b=b, g=g, t=t, ti=ti: e.dma_start(
                                x_sb[:, ti * C:(ti + 1) * C],
                                x_hbm[(b + 1) * 4 + g, :, t, :]),
                            waits=[ev_xout[(b, ti)],
                                   xin_tile.get(ti - 4,
                                                carry_xin.get(ti + 12))],
                            inc=f"dXI{ti % 4}", amt=16)
                    if ti >= 1:
                        emit_nx_stats(ti - 1)
                    if ti == 14:
                        pre_eva_nx[0] = ln_apply(0, po_nx,
                                                 carry_finish_ffn1[0])
                    elif ti == 15:
                        pre_eva_nx[1] = ln_apply(1, po_nx,
                                                 carry_finish_ffn1[0])

            hilo2 = run_ffn(2, po_f2, finish_ffn2,
                            tr_carry={0: (yfl_ev[NT - 2],),
                                      1: (yfl_ev[NT - 1],)},
                            accum_carry={(0, 0): exp_ev[len(pairs) - 1],
                                         (1, 0): evrz_ev[NT - 1],
                                         (0, 1): outp_evac[0],
                                         (1, 1): outp_evac[1]},
                            tile_hook=store_hook, pre_eva=pre_eva2)
            if b + 1 < NB:
                emit_nx_stats(NT - 1)
            carry_ffn2_hilo = {0: (hilo2[(NT - 2, "hi")], hilo2[(NT - 2, "lo")]),
                               1: (hilo2[(NT - 1, "hi")], hilo2[(NT - 1, "lo")])}
            carry_accum = {(0, 0): ffn_evac[(2, NT - 2, 0)],
                           (1, 0): ffn_evac[(2, NT - 2, 1)],
                           (0, 1): ffn_evac[(2, NT - 1, 0)],
                           (1, 1): ffn_evac[(2, NT - 1, 1)]}
            carry_xin = dict(xin_tile)
            carry_pre_eva = pre_eva_nx

        finals = [Ev(f"dXO{k}", S.cnt[f"dXO{k}"], None) for k in range(4)]
        S.op("sync", None, waits=finals)

        def mk_body(name):
            def body(eng):
                hi = {}
                for (e, waits, fn, inc, amt) in S.ops:
                    if e != name:
                        continue
                    for ev in waits:
                        if hi.get(ev.sem, 0) >= ev.val:
                            continue
                        eng.wait_ge(sem[ev.sem], ev.val)
                        hi[ev.sem] = ev.val
                    if fn is not None:
                        ins = fn(eng)
                        if inc is not None:
                            ins.then_inc(sem[inc], amt)
            return body

        block.tensor(mk_body("tensor"))
        block.vector(mk_body("vector"))
        block.scalar(mk_body("scalar"))
        block.gpsimd(mk_body("gpsimd"))
        block.sync(mk_body("sync"))

    nc._sched = S
    return nc


def kernel(L0_attn_hidden_states, L0_attn_W_basis, L0_attn_W_v, L0_attn_b_v,
           L0_attn_W_o, L0_attn_b_o, L0_ffn_weight):
    x = np.ascontiguousarray(np.asarray(L0_attn_hidden_states, dtype=np.float32))
    wv_t = np.ascontiguousarray(np.asarray(L0_attn_W_v, np.float32).T)
    wo_t = np.ascontiguousarray(np.asarray(L0_attn_W_o, np.float32).T)
    wb_t = np.ascontiguousarray(np.asarray(L0_attn_W_basis, np.float32).T)
    w_t = np.ascontiguousarray(np.asarray(L0_ffn_weight, np.float32).T)
    bv = np.ascontiguousarray(np.asarray(L0_attn_b_v, np.float32).reshape(1, C))
    bo = np.ascontiguousarray(np.asarray(L0_attn_b_o, np.float32).reshape(1, C))
    ident = np.eye(128, dtype=np.float32)
    tri = np.triu(np.ones((128, 128), dtype=np.float32))

    nc = build_program()
    in_maps = []
    for c in range(8):
        xc = np.ascontiguousarray(x[c * NB:(c + 1) * NB].reshape(NB * T, C))
        in_maps.append({"x_in": xc, "wv_t": wv_t, "wo_t": wo_t, "wb_t": wb_t,
                        "w_t": w_t, "bv": bv, "bo": bo, "ident": ident,
                        "triu": tri})
    res = run_bass_kernel_spmd(nc, in_maps, core_ids=list(range(8)))
    out = np.concatenate([np.asarray(r["y_out"]).reshape(NB, T, C)
                          for r in res.results], axis=0)
    return out.astype(np.float32, copy=False)


# version 38
# speedup vs baseline: 1.1565x; 1.1565x over previous
import numpy as np
import concourse.bass as bass
import concourse.mybir as mybir
from concourse.bass_utils import run_bass_kernel_spmd

F32 = mybir.dt.float32
BF16 = mybir.dt.bfloat16
AF = mybir.ActivationFunctionType
ALU = mybir.AluOpType
AX = mybir.AxisListType

NB = 4
T = 2048
NT = 16
C = 1024
NCH = 8
R = 64
EPS = 1e-5


class Ev:
    __slots__ = ("sem", "val", "eng")

    def __init__(self, sem, val, eng):
        self.sem, self.val, self.eng = sem, val, eng


class Sched:
    def __init__(self):
        self.ops = []
        self.cnt = {}

    def op(self, eng, fn, waits=(), inc=None, amt=1):
        w = tuple(e for e in waits if e is not None)
        ev = None
        if inc is not None:
            self.cnt[inc] = self.cnt.get(inc, 0) + amt
            ev = Ev(inc, self.cnt[inc], eng if amt == 1 else None)
        self.ops.append((eng, w, fn, inc, amt))
        return ev


def build_program():
    nc = bass.Bass("TRN2", target_bir_lowering=False)

    x_in = nc.dram_tensor("x_in", [NB * T, C], F32, kind="ExternalInput").ap()
    wv_t = nc.dram_tensor("wv_t", [C, C], F32, kind="ExternalInput").ap()
    wo_t = nc.dram_tensor("wo_t", [C, C], F32, kind="ExternalInput").ap()
    wb_t = nc.dram_tensor("wb_t", [C, R], F32, kind="ExternalInput").ap()
    w_t = nc.dram_tensor("w_t", [C, C], F32, kind="ExternalInput").ap()
    bv_h = nc.dram_tensor("bv", [1, C], F32, kind="ExternalInput").ap()
    bo_h = nc.dram_tensor("bo", [1, C], F32, kind="ExternalInput").ap()
    id_h = nc.dram_tensor("ident", [128, 128], F32, kind="ExternalInput").ap()
    tri_h = nc.dram_tensor("triu", [128, 128], F32, kind="ExternalInput").ap()
    y_out = nc.dram_tensor("y_out", [NB * T, C], F32, kind="ExternalOutput").ap()
    v_scr = nc.dram_tensor("v_scr", [T, 2 * C], BF16, kind="Internal").ap()
    wvl_scr = nc.dram_tensor("wvl_scr", [128, 2 * NCH * C], BF16, kind="Internal").ap()
    wol_scr = nc.dram_tensor("wol_scr", [128, 2 * NCH * C], BF16, kind="Internal").ap()

    S = Sched()

    from contextlib import ExitStack
    with ExitStack() as _st:
        x_sb = _st.enter_context(nc.sbuf_tensor("x_sb", [128, NT * C], F32))
        swt = _st.enter_context(nc.sbuf_tensor("swt", [128, NCH * C], BF16))
        warena = _st.enter_context(nc.sbuf_tensor("warena", [128, 2 * NCH * C], BF16))
        wbth = _st.enter_context(nc.sbuf_tensor("wbth", [128, NCH * R], BF16))
        wbtl = _st.enter_context(nc.sbuf_tensor("wbtl", [128, NCH * R], BF16))
        bvb = _st.enter_context(nc.sbuf_tensor("bvb", [128, C], F32))
        bob = _st.enter_context(nc.sbuf_tensor("bob", [128, C], F32))
        ident = _st.enter_context(nc.sbuf_tensor("ident_sb", [128, 128], F32))
        triu = _st.enter_context(nc.sbuf_tensor("triu_sb", [128, 128], F32))
        bvrow = _st.enter_context(nc.sbuf_tensor("bvrow", [1, C], F32))
        borow = _st.enter_context(nc.sbuf_tensor("borow", [1, C], F32))
        onec = _st.enter_context(nc.sbuf_tensor("onec", [128, 1], F32))
        onecb = _st.enter_context(nc.sbuf_tensor("onecb", [128, 1], BF16))
        oner = _st.enter_context(nc.sbuf_tensor("oner", [1, 128], F32))
        sums = _st.enter_context(nc.sbuf_tensor("sums", [128, 2 * NT], F32))
        sqs = _st.enter_context(nc.sbuf_tensor("sqs", [128, 2 * NT], F32))
        mvec = _st.enter_context(nc.sbuf_tensor("mvec", [128, 2 * NT], F32))
        msq = _st.enter_context(nc.sbuf_tensor("msq", [128, 2 * NT], F32))
        varv = _st.enter_context(nc.sbuf_tensor("varv", [128, 2 * NT], F32))
        sqv = _st.enter_context(nc.sbuf_tensor("sqv", [128, 2 * NT], F32))
        rstd = _st.enter_context(nc.sbuf_tensor("rstd", [128, 2 * NT], F32))
        dump = _st.enter_context(nc.sbuf_tensor("dump", [128, C], F32))
        xh32 = _st.enter_context(nc.sbuf_tensor("xh32", [128, 2 * C], F32))
        xhi = _st.enter_context(nc.sbuf_tensor("xhi", [128, 2 * C], BF16))
        xlo = _st.enter_context(nc.sbuf_tensor("xlo", [128, 2 * C], BF16))
        lath = _st.enter_context(nc.sbuf_tensor("lath", [64, T], BF16))
        latl = _st.enter_context(nc.sbuf_tensor("latl", [64, T], BF16))
        mx4 = _st.enter_context(nc.sbuf_tensor("mx4", [1, 4], F32))
        mxrow = _st.enter_context(nc.sbuf_tensor("mxrow", [1, 1], F32))
        cbias = _st.enter_context(nc.sbuf_tensor("cbias", [128, 1], F32))
        e_sb = _st.enter_context(nc.sbuf_tensor("e_sb", [128, 3 * 128], F32))
        ehb = _st.enter_context(nc.sbuf_tensor("ehb", [128, 3 * 128], BF16))
        elb = _st.enter_context(nc.sbuf_tensor("elb", [128, 3 * 128], BF16))
        vwl = _st.enter_context(nc.sbuf_tensor("vwl", [128, 2 * 2 * C], BF16))
        vr = _st.enter_context(nc.sbuf_tensor("vr", [128, 4 * 2 * C], BF16))
        y32 = _st.enter_context(nc.sbuf_tensor("y32", [128, C], F32))
        yfml = _st.enter_context(nc.sbuf_tensor("yfml", [128, 2 * C], BF16))
        rz = _st.enter_context(nc.sbuf_tensor("rz", [128, 2], F32))
        xsp = _st.enter_context(nc.sbuf_tensor("xsp", [128, C], F32))
        tmpo = _st.enter_context(nc.sbuf_tensor("tmpo", [128, C], F32))
        psB = _st.enter_context(nc.psum_tensor("psB", [128, 512], F32))
        psC = _st.enter_context(nc.psum_tensor("psC", [128, 512], F32))
        psD = _st.enter_context(nc.psum_tensor("psD", [128, 1024], F32))
        psD2 = _st.enter_context(nc.psum_tensor("psD2", [128, 1024], F32))
        psE = _st.enter_context(nc.psum_tensor("psE", [128, 512], F32))
        psF = _st.enter_context(nc.psum_tensor("psF", [128, 512], F32))
        bank = lambda i: psD if i % 2 == 0 else psD2
        sem = {}
        for nm in (["T", "V", "S", "G", "dWG", "dWV", "dWO"]
                   + [f"dIN{i}" for i in range(4)]
                   + [f"dVO{i}" for i in range(2)]
                   + [f"dXI{i}" for i in range(4)]
                   + [f"dXO{i}" for i in range(4)]
                   + [f"dVG{i}" for i in range(4)]
                   + [f"dSC{i}" for i in range(2)]):
            sem[nm] = _st.enter_context(nc.semaphore("sem_" + nm))
        block = _st.enter_context(nc.Block())

        for dst, srch in ((triu, tri_h), (bvrow, bv_h),
                          (borow, bo_h)):
            S.op("gpsimd",
                 lambda e, d=dst, s=srch: e.dma_start(d[:, :], s[:, :]),
                 inc="dWG", amt=16)
        ev_consts_all = S.op("gpsimd",
                             lambda e: e.dma_start(ident[:, :], id_h[:, :]),
                             inc="dWG", amt=16)
        ev_ones = S.op("vector", lambda e: e.memset(onec[:, :], 1.0), inc="V")
        ev_onesb = S.op("vector", lambda e: e.memset(onecb[:, :], 1.0), inc="V")
        ev_oner = S.op("vector", lambda e: e.memset(oner[:, :], 1.0), inc="V")

        ev_bb = {}
        for bi, (row, tile) in enumerate(((bvrow, bvb), (borow, bob))):
            for h, ps in ((0, psB), (1, psC)):
                evm = S.op("tensor",
                           lambda e, r=row, h=h, ps=ps: e.matmul(
                               ps[:, 0:512], oner[0:1, :], r[0:1, h * 512:(h + 1) * 512],
                               start=True, stop=True),
                           waits=[ev_oner, ev_consts_all] + ([ev_bb.get((bi - 1, h))] if bi else []),
                           inc="T")
                ev_bb[(bi, h)] = S.op("vector",
                                      lambda e, t=tile, h=h, ps=ps: e.tensor_copy(
                                          t[:, h * 512:(h + 1) * 512], ps[:, 0:512]),
                                      waits=[evm], inc="V")

        LO = NCH * C
        stg_bufs = [(tmpo, 0), (xsp, 0), (dump, 0), (y32, 0)]
        stg_free = [None, None, None, None]
        last_dIN = {}

        def stage4(i, src_slice):
            buf, off = stg_bufs[i]
            eng, sm = (("sync", f"dIN{i}") if i < 2
                       else ("scalar", f"dSC{i - 2}"))
            evd = S.op(eng,
                       lambda e, s=src_slice, buf=buf, off=off: e.dma_start(
                           buf[:, off:off + C], s),
                       waits=[stg_free[i]], inc=sm, amt=16)
            if i < 2:
                last_dIN[i] = evd
            return evd

        ev_sign = [None] * NCH
        dma_ev = {}
        for ic in range(4):
            dma_ev[ic] = stage4(ic, w_t[ic * 128:(ic + 1) * 128, :])
        for ic in range(NCH):
            evd = dma_ev[ic]
            buf, off = stg_bufs[ic % 4]
            ev_sign[ic] = S.op("scalar",
                               lambda e, ic=ic, buf=buf, off=off: e.activation(
                                   swt[:, ic * C:(ic + 1) * C],
                                   buf[:, off:off + C], AF.Sign),
                               waits=[evd], inc="S")
            stg_free[ic % 4] = ev_sign[ic]
            if ic + 4 < NCH:
                dma_ev[ic + 4] = stage4(
                    ic, w_t[(ic + 4) * 128:(ic + 5) * 128, :])

        wbev = {}

        def emit_wb():
            evd_wb = S.op("sync",
                          lambda e: e.dma_start(
                              tmpo[:, 0:512].rearrange("p (ic r) -> p ic r",
                                                       ic=NCH),
                              wb_t.rearrange("(ic p) r -> p ic r", p=128)),
                          waits=[stg_free[0]], inc="dIN0", amt=16)
            last_dIN[0] = evd_wb
            wbev["h"] = S.op("scalar",
                             lambda e: e.activation(wbth[:, :], tmpo[:, 0:512],
                                                    AF.Copy),
                             waits=[evd_wb], inc="S")
            wbev["l"] = S.op("vector",
                             lambda e: e.tensor_sub(wbtl[:, :], tmpo[:, 0:512],
                                                    wbth[:, :]),
                             waits=[wbev["h"]], inc="V")
            stg_free[0] = wbev["l"]

        init_free = [None, None]
        ev_spill = {}
        wv_ready = [None]

        def emit_wchunk(j):
            if j == 0:
                init_free[0], init_free[1] = stg_free[0], stg_free[1]
            w_src, ic = (wo_t, j) if j < NCH else (wv_t, j - NCH)
            slot = j % 2
            buf, off = stg_bufs[slot]
            evd = S.op("sync",
                       lambda e, s=w_src[ic * 128:(ic + 1) * 128, :],
                       buf=buf, off=off: e.dma_start(
                           buf[:, off:off + C], s),
                       waits=[init_free[slot]], inc=f"dIN{slot}", amt=16)
            last_dIN[slot] = evd
            evh = S.op("scalar",
                       lambda e, ic=ic, buf=buf, off=off: e.activation(
                           warena[:, ic * C:(ic + 1) * C],
                           buf[:, off:off + C], AF.Copy),
                       waits=[evd, ev_spill.get("wo")], inc="S")
            evl = S.op("vector",
                       lambda e, ic=ic, buf=buf, off=off: e.tensor_sub(
                           warena[:, LO + ic * C:LO + (ic + 1) * C],
                           buf[:, off:off + C],
                           warena[:, ic * C:(ic + 1) * C]),
                       waits=[evh], inc="V")
            init_free[slot] = evl
            if j == NCH - 1:
                ev_spill["wo"] = S.op("sync",
                                      lambda e: e.dma_start(wol_scr[:, :],
                                                            warena[:, :]),
                                      waits=[evl], inc="dWV", amt=16)
            elif j == 2 * NCH - 1:
                wv_ready[0] = evl
                ev_spill["wv"] = S.op("sync",
                                      lambda e: e.dma_start(wvl_scr[:, :],
                                                            warena[:, :]),
                                      waits=[evl], inc="dWV", amt=16)

        ev_xout = {}
        ev_wvt = [None, None]
        ev_wot = None
        ev_outp_last = None
        carry_ffn2_hilo = {0: (ev_consts_all,), 1: (ev_consts_all,)}
        carry_accum = {(0, 0): None, (1, 0): None,
                       (0, 1): ev_bb[(1, 0)], (1, 1): ev_bb[(1, 1)]}
        carry_evrz15 = None
        carry_xin = {}
        carry_flush = None
        carry_pre_eva = {}
        x_hbm = x_in.rearrange("(k t p) c -> k p t c", p=128, t=4)
        y_hbm = y_out.rearrange("(k t p) c -> k p t c", p=128, t=4)

        def sb4(k):
            return x_sb[:, k * 4 * C:(k + 1) * 4 * C].rearrange(
                "p (t c) -> p t c", t=4)

        evsq_rd_done = [stg_free[2]]
        tr_ev = [None, None]

        def emit_stat_tile(po, ti, gates, src=None):
            sv = src if src is not None else x_sb[:, ti * C:(ti + 1) * C]
            evsum = S.op("vector",
                         lambda e, po=po, ti=ti, sv=sv: e.reduce_sum(
                             sums[:, po + ti:po + ti + 1], sv, axis=AX.X),
                         waits=gates, inc="V")
            evsq = S.op("scalar",
                        lambda e, ti=ti, sv=sv: e.activation(
                            dump[:, :], sv, AF.Square),
                        waits=list(gates) + [evsq_rd_done[0]], inc="S")
            evsq2 = S.op("vector",
                         lambda e, po=po, ti=ti: e.reduce_sum(
                             sqs[:, po + ti:po + ti + 1], dump[:, :], axis=AX.X),
                         waits=[evsq], inc="V")
            evsq_rd_done[0] = evsq2
            return [evsum, evsq2]

        def emit_finish(po, g, stat_evs):
            c0 = po + 4 * g
            sl = slice(c0, c0 + 4)
            evm0 = S.op("vector",
                        lambda e, sl=sl: e.tensor_scalar_mul(
                            mvec[:, sl], sums[:, sl], 1.0 / C),
                        waits=stat_evs, inc="V")
            evm1 = S.op("vector",
                        lambda e, sl=sl: e.tensor_mul(msq[:, sl], mvec[:, sl],
                                                      mvec[:, sl]),
                        waits=[evm0], inc="V")
            evv = S.op("vector",
                       lambda e, sl=sl: e.tensor_scalar(
                           varv[:, sl], sqs[:, sl], 1.0 / C, None, ALU.mult),
                       waits=stat_evs, inc="V")
            evv2 = S.op("vector",
                        lambda e, sl=sl: e.tensor_sub(varv[:, sl], varv[:, sl],
                                                      msq[:, sl]),
                        waits=[evm1, evv], inc="V")
            evv3 = S.op("vector",
                        lambda e, sl=sl: e.tensor_scalar_add(varv[:, sl],
                                                             varv[:, sl], EPS),
                        waits=[evv2], inc="V")
            evsq = S.op("scalar",
                        lambda e, sl=sl: e.activation(sqv[:, sl], varv[:, sl],
                                                      AF.Sqrt),
                        waits=[evv3], inc="S")
            evr = S.op("vector",
                       lambda e, sl=sl: e.reciprocal(rstd[:, sl], sqv[:, sl]),
                       waits=[evsq], inc="V")
            return evr

        ev_xin0 = {}
        for ti in range(NT):
            g, t = ti // 4, ti % 4
            ev_xin0[ti] = S.op("gpsimd",
                               lambda e, g=g, t=t: e.dma_start(
                                   x_sb[:, (4 * g + t) * C:
                                        (4 * g + t + 1) * C],
                                   x_hbm[g, :, t, :]),
                               waits=[ev_xin0.get(ti - 4)],
                               inc=f"dXI{ti % 4}", amt=16)
        finish_ffn1_b0 = {}
        b0_sevs = {}

        def emit_b0_tile(tj):
            g = tj // 4
            sevs = b0_sevs.setdefault(g, [])
            sevs += emit_stat_tile(0, tj, [ev_xin0[tj]])
            if tj % 4 == 3:
                finish_ffn1_b0[g] = emit_finish(0, g, sevs)

        for tj in range(8):
            emit_b0_tile(tj)
        carry_xin = dict(ev_xin0)

        for b in range(NB):
            finish_ffn1 = finish_ffn1_b0 if b == 0 else carry_finish_ffn1

            def ln_apply(ti, po, gate):
                sl = (ti % 2) * C
                war = [tr_ev[(ti - 2) % 2]] if tr_ev[(ti - 2) % 2] else []
                return S.op("vector",
                            lambda e, ti=ti, sl=sl, po=po: e.tensor_scalar(
                                xh32[:, sl:sl + C], x_sb[:, ti * C:(ti + 1) * C],
                                mvec[:, po + ti:po + ti + 1],
                                rstd[:, po + ti:po + ti + 1],
                                ALU.subtract, ALU.mult),
                            waits=[gate] + war,
                            inc="V")

            def ln_transpose(ti, eva, tr_free_evs):
                sl = (ti % 2) * C

                def fn(e, sl=sl, bk=bank(ti)):
                    last = None
                    for ic in range(NCH):
                        last = e.transpose(bk[:, ic * 128:(ic + 1) * 128],
                                           xh32[:, sl + ic * 128:sl + (ic + 1) * 128],
                                           ident[:, :])
                    return last
                evt = S.op("tensor", fn, waits=[eva] + tr_free_evs, inc="T")
                tr_ev[ti % 2] = evt
                return evt

            def run_ffn(sub, po, finish_evs, tr_carry, accum_carry,
                        tile_hook=None, inject_at2=None, pre_eva=None):
                evac_ev = dict(accum_carry)
                hilo_ev = {}
                mm_ev = {}
                trs = {}
                eva_d = {}

                def emit_eva(ti):
                    if ti >= NT:
                        return
                    if pre_eva and ti in pre_eva:
                        eva_d[ti] = pre_eva[ti]
                        return
                    eva_d[ti] = ln_apply(ti, po, finish_evs[ti // 4])

                def emit_tr(ti):
                    if ti >= NT:
                        return
                    free = ([hilo_ev[(ti - 2, "hi")], hilo_ev[(ti - 2, "lo")]]
                            if ti >= 2 else list(tr_carry.get(ti % 2, ())))
                    trs[ti] = ln_transpose(ti, eva_d[ti],
                                           [f for f in free if f])
                def emit_hilo(ti):
                    if ti >= NT:
                        return
                    evt = trs[ti]
                    sl = (ti % 2) * C
                    bk = bank(ti)
                    war = [mm_ev[(ti - 2, 1)]] if (ti - 2, 1) in mm_ev else []
                    ehi = S.op("scalar",
                               lambda e, sl=sl, bk=bk: e.activation(
                                   xhi[:, sl:sl + C], bk[:, :], AF.Copy),
                               waits=[evt] + war, inc="S")
                    elo = S.op("vector",
                               lambda e, sl=sl, bk=bk: e.tensor_sub(
                                   xlo[:, sl:sl + C], bk[:, :],
                                   xhi[:, sl:sl + C]),
                               waits=[evt, ehi] + war, inc="V")
                    hilo_ev[(ti, "hi")] = ehi
                    hilo_ev[(ti, "lo")] = elo
                emit_eva(0)
                emit_eva(1)
                emit_tr(0)
                emit_eva(2)
                emit_tr(1)
                emit_hilo(0)
                for ti in range(NT):
                    if ti == 6 and inject_at2 is not None:
                        inject_at2()
                    emit_eva(ti + 3)
                    emit_tr(ti + 2)
                    emit_hilo(ti + 1)
                    ehi = hilo_ev[(ti, "hi")]
                    elo = hilo_ev[(ti, "lo")]
                    sl = (ti % 2) * C
                    par = ti % 2
                    for h, ps in ((0, psE if par == 0 else psB),
                                  (1, psF if par == 0 else psC)):
                        def fn(e, sl=sl, h=h, ps=ps):
                            last = None
                            for j, src in enumerate((xhi, xlo)):
                                for ic in range(NCH):
                                    last = e.matmul(
                                        ps[:, 0:512],
                                        src[:, sl + ic * 128:sl + (ic + 1) * 128],
                                        swt[:, ic * C + h * 512:ic * C + (h + 1) * 512],
                                        start=(j == 0 and ic == 0),
                                        stop=(j == 1 and ic == NCH - 1))
                            return last
                        evm = S.op("tensor", fn,
                                   waits=[ehi, elo, evac_ev[(h, par)],
                                          ev_sign[7]], inc="T")
                        mm_ev[(ti, h)] = evm
                        evac_ev[(h, par)] = S.op(
                            "vector",
                            lambda e, ti=ti, h=h, ps=ps:
                            e.scalar_tensor_tensor(
                                x_sb[:, ti * C + h * 512:ti * C + (h + 1) * 512],
                                ps[:, 0:512], 0.5,
                                x_sb[:, ti * C + h * 512:ti * C + (h + 1) * 512],
                                ALU.mult, ALU.add),
                            waits=[evm], inc="V")
                        ffn_evac[(sub, ti, h)] = evac_ev[(h, par)]
                    if tile_hook is not None:
                        tile_hook(ti)
                return hilo_ev

            ffn_evac = {}

            po_f1 = (b % 2) * NT
            po_at = ((b + 1) % 2) * NT
            po_f2 = (b % 2) * NT
            finish_attn = {}

            at_stats = {}

            def attn_hook(ti):
                if b == 0:
                    if ti == 0:
                        emit_wb()
                    if ti < 8:
                        emit_b0_tile(8 + ti)
                sevs = at_stats.setdefault(ti // 4, [])
                sevs += emit_stat_tile(
                    po_at, ti, [ffn_evac[(0, ti, 0)], ffn_evac[(0, ti, 1)]])
                if ti % 4 == 3:
                    finish_attn[ti // 4] = emit_finish(po_at, ti // 4, sevs)
                if b == 0:
                    emit_wchunk(ti)

            hilo1 = run_ffn(0, po_f1, finish_ffn1,
                            tr_carry=carry_ffn2_hilo, accum_carry=carry_accum,
                            tile_hook=attn_hook, inject_at2=carry_flush,
                            pre_eva=carry_pre_eva)

            if b == 0:
                ev_wvt = [None, wv_ready[0]]
            else:
                ev_wvt = []
                for k in range(2):
                    ev_wvt.append(S.op("sync",
                                       lambda e, k=k: e.dma_start(
                                           warena[:, k * LO:(k + 1) * LO],
                                           wvl_scr[:, k * LO:(k + 1) * LO]),
                                       waits=[ev_outp_last, ev_spill["wv"]],
                                       inc="dWV", amt=16))

            lat_ev = None
            xfm_ev = {}
            vmm_done = {}
            vh_ev = {}
            vl_ev = {}
            vout_ev = {}
            atr = {}

            aeva_d = {}

            def emit_aeva(ti):
                if ti >= NT:
                    return
                aeva_d[ti] = ln_apply(ti, po_at, finish_attn[ti // 4])

            def emit_atr(ti):
                if ti >= NT:
                    return
                free = ([xfm_ev[ti - 2]] if ti >= 2 else
                        [hilo1.get((14 + ti, "hi")), hilo1.get((14 + ti, "lo"))])
                atr[ti] = ln_transpose(ti, aeva_d[ti], [f for f in free if f])
            evxh_d = {}

            def emit_bx(ti):
                if ti >= NT:
                    return
                evt = atr[ti]
                sl = (ti % 2) * C
                bk = bank(ti)
                free = [vmm_done[ti - 2]] if ti - 2 in vmm_done else []
                evxh = S.op("scalar",
                            lambda e, sl=sl, bk=bk: e.activation(
                                xhi[:, sl:sl + C], bk[:, :], AF.Copy),
                            waits=[evt] + free, inc="S")
                evxl = S.op("vector",
                            lambda e, sl=sl, bk=bk: e.tensor_sub(
                                xlo[:, sl:sl + C], bk[:, :],
                                xhi[:, sl:sl + C]),
                            waits=[evxh], inc="V")
                evxh_d[ti] = evxh
                xfm_ev[ti] = evxl
            emit_aeva(0)
            emit_aeva(1)
            emit_atr(0)
            emit_aeva(2)
            emit_atr(1)
            emit_bx(0)
            for ti in range(NT):
                emit_aeva(ti + 3)
                emit_atr(ti + 2)
                emit_bx(ti + 1)
                evxh = evxh_d[ti]
                evxl = xfm_ev[ti]
                sl = (ti % 2) * C

                def fn_lat(e, sl=sl, ti=ti):
                    last = None
                    for ic in range(NCH):
                        for j, (wl, src) in enumerate(
                                ((wbth, xhi), (wbtl, xhi), (wbth, xlo))):
                            last = e.matmul(
                                psF[0:64, 0:128],
                                wl[:, ic * R:(ic + 1) * R],
                                src[:, sl + ic * 128:sl + (ic + 1) * 128],
                                start=(ic == 0 and j == 0),
                                stop=(ic == NCH - 1 and j == 2))
                    return last
                evlm = S.op("tensor", fn_lat,
                            waits=[evxh, evxl, wbev["h"], wbev["l"],
                                   lat_ev if ti else
                                   ffn_evac[(0, NT - 2, 1)]], inc="T")
                lh_ev = S.op("scalar",
                             lambda e, ti=ti: e.activation(
                                 lath[0:64, ti * 128:(ti + 1) * 128],
                                 psF[0:64, 0:128], AF.Copy),
                             waits=[evlm], inc="S")
                lat_ev = S.op("vector",
                              lambda e, ti=ti: e.tensor_sub(
                                  latl[0:64, ti * 128:(ti + 1) * 128],
                                  psF[0:64, 0:128],
                                  lath[0:64, ti * 128:(ti + 1) * 128]),
                              waits=[lh_ev], inc="V")

                sl3 = (ti % 2) * 2 * C
                for h, ps in ((0, psB), (1, psC)):
                    def fn_v(e, sl=sl, h=h, ps=ps):
                        last = None
                        for ic in range(NCH):
                            for j, (src, woff) in enumerate(
                                    ((xhi, 0), (xhi, LO), (xlo, 0))):
                                last = e.matmul(
                                    ps[:, 0:512],
                                    src[:, sl + ic * 128:sl + (ic + 1) * 128],
                                    warena[:, woff + ic * C + h * 512:
                                           woff + ic * C + (h + 1) * 512],
                                    start=(ic == 0 and j == 0),
                                    stop=(ic == NCH - 1 and j == 2))
                        return last
                    evm = S.op("tensor", fn_v,
                               waits=[evxh, evxl, ev_wvt[1],
                                      vl_ev.get((ti - 1, h)),
                                      ffn_evac[(0, NT - 1, h)]],
                               inc="T")
                    vmm_done[ti] = evm
                    vh_ev[(ti, h)] = S.op(
                        "scalar",
                        lambda e, sl3=sl3, h=h, ps=ps: e.activation(
                            vwl[:, sl3 + h * 512:sl3 + (h + 1) * 512],
                            ps[:, 0:512], AF.Copy),
                        waits=[evm, vout_ev.get(ti - 2)], inc="S")
                    vl_ev[(ti, h)] = S.op(
                        "vector",
                        lambda e, sl3=sl3, h=h, ps=ps: e.tensor_sub(
                            vwl[:, sl3 + C + h * 512:sl3 + C + (h + 1) * 512],
                            ps[:, 0:512],
                            vwl[:, sl3 + h * 512:sl3 + (h + 1) * 512]),
                        waits=[vh_ev[(ti, h)]], inc="V")
                vout_ev[ti] = S.op("scalar",
                                   lambda e, ti=ti, sl3=sl3: e.dma_start(
                                       v_scr[ti * 128:(ti + 1) * 128, :],
                                       vwl[:, sl3:sl3 + 2 * C]),
                                   waits=[vl_ev[(ti, 0)], vl_ev[(ti, 1)]],
                                   inc=f"dVO{ti % 2}", amt=16)

            n2_banks = [psB, psC, psF, psD]
            n2_ev = []
            prev_evac = None
            for j in range(4):
                eva2 = S.op("vector",
                            lambda e, j=j: e.tensor_add(
                                dump[0:64, 512:1024],
                                lath[0:64, j * 512:(j + 1) * 512],
                                latl[0:64, j * 512:(j + 1) * 512]),
                            waits=[lat_ev, prev_evac, evsq_rd_done[0]],
                            inc="V")
                evq = S.op("scalar",
                           lambda e: e.activation(
                               dump[0:64, 512:1024], dump[0:64, 512:1024],
                               AF.Square),
                           waits=[eva2], inc="S")
                wait_bank = []
                if n2_banks[j] is psB:
                    wait_bank = [vl_ev[(NT - 1, 0)]]
                elif n2_banks[j] is psC:
                    wait_bank = [vl_ev[(NT - 1, 1)]]
                elif n2_banks[j] is psD:
                    wait_bank = [xfm_ev[NT - 2]]
                evm = S.op("tensor",
                           lambda e, j=j: e.matmul(
                               n2_banks[j][0:1, 0:512], onec[0:64, 0:1],
                               dump[0:64, 512:1024], start=True, stop=True),
                           waits=[evq, ev_ones] + wait_bank, inc="T")
                prev_evac = S.op("vector",
                                 lambda e, j=j: e.reduce_max(
                                     mx4[0:1, j:j + 1],
                                     n2_banks[j][0:1, 0:512], axis=AX.X),
                                 waits=[evm], inc="V")
                n2_ev.append(prev_evac)
            evmx = S.op("vector",
                        lambda e: e.reduce_max(mxrow[0:1, 0:1], mx4[0:1, :],
                                               axis=AX.X),
                        waits=n2_ev, inc="V")
            evcb_m = S.op("tensor",
                          lambda e: e.matmul(psE[:, 0:1], oner[0:1, :],
                                             mxrow[0:1, 0:1], start=True, stop=True),
                          waits=[evmx, ev_oner], inc="T")
            evcb = S.op("vector",
                        lambda e: e.tensor_scalar_mul(cbias[:, :], psE[:, 0:1],
                                                      -0.125),
                        waits=[evcb_m], inc="V")

            ev_wot = []
            for k in range(2):
                ev_wot.append(S.op("sync",
                                   lambda e, k=k: e.dma_start(
                                       warena[:, k * LO:(k + 1) * LO],
                                       wol_scr[:, k * LO:(k + 1) * LO]),
                                   waits=[vmm_done[NT - 1]] +
                                         ([ev_spill["wv"]] if b == 0 else []),
                                   inc="dWO", amt=16))

            pairs = [(qi, ki) for qi in range(NT) for ki in range(qi + 1)]
            vin_ev = {}
            ving_ev = {}
            emitted = [0]

            def emit_vin(n):
                while emitted[0] < min(n, len(pairs)):
                    idx = emitted[0]
                    qi, ki = pairs[idx]
                    slot = idx % 4
                    wait_slot = ([yz_ev[idx - 4]] if idx >= 4 else
                                 [last_dIN.get(slot)])
                    vin_ev[idx] = S.op(
                        "sync",
                        lambda e, ki=ki, slot=slot: e.dma_start(
                            vr[:, slot * 2 * C:slot * 2 * C + C],
                            v_scr[ki * 128:(ki + 1) * 128, 0:C]),
                        waits=wait_slot + [vout_ev[ki]],
                        inc=f"dIN{slot}", amt=16)
                    ving_ev[idx] = S.op(
                        "gpsimd",
                        lambda e, ki=ki, slot=slot: e.dma_start(
                            vr[:, slot * 2 * C + C:(slot + 1) * 2 * C],
                            v_scr[ki * 128:(ki + 1) * 128, C:2 * C]),
                        waits=([yz_ev[idx - 4]] if idx >= 4 else []) +
                              [vout_ev[ki]],
                        inc=f"dVG{slot}", amt=16)
                    emitted[0] += 1

            yz_ev = {}
            exp_ev = {}
            scr = {}
            yfl_ev = {}
            ytr_ev = {}
            evrz_ev = {}
            outp_evac = {0: None, 1: None}
            outp_holder = [None]
            oev_t = {}
            bob_pre = {}
            pend_tail = []
            finish_ffn2 = {}
            f2stats = {}
            emit_vin(4)

            def emit_f2_stats(ti):
                sevs = f2stats.setdefault(ti // 4, [])
                sevs += emit_stat_tile(po_f2, ti,
                                       [oev_t[(ti, 0)], oev_t[(ti, 1)]])
                if ti % 4 == 3:
                    finish_ffn2[ti // 4] = emit_finish(po_f2, ti // 4, sevs)

            def emit_score(idx):
                if idx >= len(pairs):
                    return
                qi, ki = pairs[idx]
                slot = idx % 3
                esl = 128 + slot * 128
                es = slot * 128
                def fn_st(e, qi=qi, ki=ki, esl=esl):
                    last = None
                    for j, (kl, ql) in enumerate(
                            ((lath, lath), (lath, latl), (latl, lath))):
                        last = e.matmul(
                            psE[:, esl:esl + 128],
                            kl[0:64, ki * 128:(ki + 1) * 128],
                            ql[0:64, qi * 128:(qi + 1) * 128],
                            start=(j == 0), stop=(j == 2))
                    return last
                evst = S.op("tensor", fn_st,
                            waits=[lat_ev, evcb, exp_ev.get(idx - 1)], inc="T")
                eve = S.op("scalar",
                           lambda e, esl=esl, es=es: e.activation(
                               e_sb[:, es:es + 128], psE[:, esl:esl + 128],
                               AF.Exp, bias=cbias[:, 0:1], scale=0.125),
                           waits=[evst, evcb, yz_ev.get(idx - 3)], inc="S")
                exp_ev[idx] = eve
                gate = eve
                if ki == qi:
                    gate = S.op("vector",
                                lambda e, es=es: e.tensor_mul(
                                    e_sb[:, es:es + 128], e_sb[:, es:es + 128],
                                    triu[:, :]),
                                waits=[eve, ev_consts_all], inc="V")
                evh = S.op("vector",
                           lambda e, es=es: e.tensor_copy(
                               ehb[:, es:es + 128], e_sb[:, es:es + 128]),
                           waits=[gate], inc="V")
                evl = S.op("vector",
                           lambda e, es=es: e.tensor_sub(
                               elb[:, es:es + 128], e_sb[:, es:es + 128],
                               ehb[:, es:es + 128]),
                           waits=[evh], inc="V")
                scr[idx] = (gate, evh, evl)

            def emit_rz(qi):
                idx_d = qi * (qi + 1) // 2 + qi
                zc = qi % 2
                evrz_ev[qi] = S.op("vector",
                                   lambda e, zc=zc: e.reciprocal(
                                       rz[:, zc:zc + 1], psF[:, 0:1]),
                                   waits=[yz_ev[idx_d]], inc="V")

            def emit_tail_a(qi):
                pd = bank(qi)
                zc = qi % 2
                bob_pre[qi] = S.op("vector",
                                   lambda e, qi=qi: e.tensor_add(
                                       x_sb[:, qi * C:(qi + 1) * C],
                                       x_sb[:, qi * C:(qi + 1) * C],
                                       bob[:, :]),
                                   waits=[aeva_d[qi], ev_bb[(1, 0)],
                                          ev_bb[(1, 1)]],
                                   inc="V")
                y32_ev = S.op("vector",
                              lambda e, pd=pd, zc=zc: e.scalar_tensor_tensor(
                                  y32[:, :], pd[:, :], rz[:, zc:zc + 1],
                                  bvb[:, :], ALU.mult, ALU.add),
                              waits=[evrz_ev[qi], ev_bb[(0, 0)], ev_bb[(0, 1)],
                                     ytr_ev.get(qi - 1)],
                              inc="V")

                def fn_ytr(e, pd=pd):
                    last = None
                    for cc in range(NCH):
                        last = e.transpose(pd[:, cc * 128:(cc + 1) * 128],
                                           y32[:, cc * 128:(cc + 1) * 128],
                                           ident[:, :])
                    return last
                ytr_ev[qi] = S.op("tensor", fn_ytr, waits=[y32_ev], inc="T")

            def emit_tail_b(qi):
                pd = bank(qi)
                evyt = ytr_ev[qi]
                yfh = S.op("scalar",
                           lambda e, pd=pd: e.activation(yfml[:, 0:C],
                                                         pd[:, :], AF.Copy),
                           waits=[evyt], inc="S")
                yfl = S.op("vector",
                           lambda e, pd=pd: e.tensor_sub(
                               yfml[:, C:2 * C], pd[:, :], yfml[:, 0:C]),
                           waits=[yfh], inc="V")
                yfl_ev[qi] = yfl
                for h, ps in ((0, psB), (1, psC)):
                    def fn_op(e, h=h, ps=ps):
                        last = None
                        for cc in range(NCH):
                            for j, (aoff, woff) in enumerate(
                                    ((0, 0), (0, LO), (C, 0))):
                                last = e.matmul(
                                    ps[:, 0:512],
                                    yfml[:, aoff + cc * 128:
                                         aoff + (cc + 1) * 128],
                                    warena[:, woff + cc * C + h * 512:
                                           woff + cc * C + (h + 1) * 512],
                                    start=(cc == 0 and j == 0),
                                    stop=(cc == NCH - 1 and j == 2))
                        return last
                    wb = [yfh, yfl, ev_wot[1], outp_evac[h]]
                    if qi == 0:
                        wb.append(n2_ev[0 if h == 0 else 1])
                    evm = S.op("tensor", fn_op, waits=wb, inc="T")
                    outp_holder[0] = evm
                    evv = S.op("vector",
                               lambda e, qi=qi, h=h, ps=ps: e.tensor_add(
                                   x_sb[:, qi * C + h * 512:
                                        qi * C + (h + 1) * 512],
                                   ps[:, 0:512],
                                   x_sb[:, qi * C + h * 512:
                                        qi * C + (h + 1) * 512]),
                               waits=[evm, bob_pre[qi]], inc="V")
                    outp_evac[h] = evv
                    oev_t[(qi, h)] = evv
                if qi >= 1:
                    emit_f2_stats(qi - 1)

            emit_score(0)
            emit_score(1)
            pre_eva2 = {}
            for idx, (qi, ki) in enumerate(pairs):
                emit_vin(idx + 4)
                while pend_tail and pend_tail[0][0] <= idx:
                    pend_tail.pop(0)[1]()
                if idx == 104:
                    pre_eva2[0] = ln_apply(0, po_f2, finish_ffn2[0])
                    pre_eva2[1] = ln_apply(1, po_f2, finish_ffn2[0])
                emit_score(idx + 2)
                gate, evh, evl = scr[idx]
                es = (idx % 3) * 128
                vslot = idx % 4
                pd = bank(qi)

                def fn_yz(e, qi=qi, ki=ki, vslot=vslot, es=es, pd=pd):
                    st = ki == 0
                    sp = ki == qi
                    base = vslot * 2 * C
                    for j, (ab, voff) in enumerate(((ehb, 0), (ehb, C), (elb, 0))):
                        e.matmul(pd[:, 0:512], ab[:, es:es + 128],
                                 vr[:, base + voff:base + voff + 512],
                                 start=(st and j == 0), stop=(sp and j == 2))
                        e.matmul(pd[:, 512:1024], ab[:, es:es + 128],
                                 vr[:, base + voff + 512:base + voff + 1024],
                                 start=(st and j == 0), stop=(sp and j == 2))
                    return e.matmul(psF[:, 0:1],
                                    e_sb[:, es:es + 128],
                                    onec[:, 0:1], start=st, stop=sp)
                w = [gate, evh, evl, vin_ev[idx], ving_ev[idx]]
                if ki == 0:
                    if qi >= 1:
                        w.append(evrz_ev[qi - 1])
                    if qi >= 2:
                        w.append(yfl_ev[qi - 2])
                    elif qi < 2:
                        w.append(xfm_ev[NT - 2 + qi])
                yz_ev[idx] = S.op("tensor", fn_yz, waits=w, inc="T")
                if ki == qi:
                    emit_rz(qi)
                    pend_tail.append((idx + 2, lambda qi=qi: emit_tail_a(qi)))
                    pend_tail.append((idx + 3, lambda qi=qi: emit_tail_b(qi)))
            for _, fn in pend_tail:
                fn()
            ev_outp_last = outp_holder[0]
            carry_evrz15 = evrz_ev[NT - 1]

            carry_finish_ffn1 = {}
            po_nx = ((b + 1) % 2) * NT
            xin_tile = {}
            nx_stats = {}
            xsp_cp = [None]
            pre_eva_nx = {}

            def emit_nx_stats(ti):
                sevs = nx_stats.setdefault(ti // 4, [])
                sevs += emit_stat_tile(
                    po_nx, ti, [xin_tile[ti]],
                    src=(xsp[:, :] if ti == NT - 1 else None))
                if ti == NT - 1:
                    sevs.append(xsp_cp[0])
                if ti % 4 == 3:
                    carry_finish_ffn1[ti // 4] = emit_finish(
                        po_nx, ti // 4, sevs)

            def store_hook(ti):
                if ti == 0:
                    emit_f2_stats(NT - 1)
                g, t = ti // 4, ti % 4
                ev_xout[(b, ti)] = S.op(
                    "gpsimd",
                    lambda e, b=b, g=g, t=t, ti=ti: e.dma_start(
                        y_hbm[b * 4 + g, :, t, :],
                        x_sb[:, ti * C:(ti + 1) * C]),
                    waits=[ffn_evac[(2, ti, 0)], ffn_evac[(2, ti, 1)]],
                    inc=f"dXO{ti % 4}", amt=16)
                if b + 1 < NB:
                    if ti == NT - 1:
                        xin_tile[ti] = S.op(
                            "gpsimd",
                            lambda e, b=b: e.dma_start(
                                xsp[:, :], x_hbm[(b + 1) * 4 + 3, :, 3, :]),
                            waits=[xin_tile.get(ti - 4,
                                                carry_xin.get(ti + 12)),
                                   finish_ffn1[3]] +
                                  ([init_free[1]] if b == 0 else []),
                            inc=f"dXI{ti % 4}", amt=16)
                        xsp_cp[0] = S.op(
                            "gpsimd",
                            lambda e, ti=ti: e.tensor_copy(
                                x_sb[:, ti * C:(ti + 1) * C], xsp[:, :]),
                            waits=[ev_xout[(b, ti)], xin_tile[ti]], inc="G")
                    else:
                        xin_tile[ti] = S.op(
                            "gpsimd",
                            lambda e, b=b, g=g, t=t, ti=ti: e.dma_start(
                                x_sb[:, ti * C:(ti + 1) * C],
                                x_hbm[(b + 1) * 4 + g, :, t, :]),
                            waits=[ev_xout[(b, ti)],
                                   xin_tile.get(ti - 4,
                                                carry_xin.get(ti + 12))],
                            inc=f"dXI{ti % 4}", amt=16)
                    if ti >= 1:
                        emit_nx_stats(ti - 1)
                    if ti == 14:
                        pre_eva_nx[0] = ln_apply(0, po_nx,
                                                 carry_finish_ffn1[0])
                    elif ti == 15:
                        pre_eva_nx[1] = ln_apply(1, po_nx,
                                                 carry_finish_ffn1[0])

            hilo2 = run_ffn(2, po_f2, finish_ffn2,
                            tr_carry={0: (yfl_ev[NT - 2],),
                                      1: (yfl_ev[NT - 1],)},
                            accum_carry={(0, 0): exp_ev[len(pairs) - 1],
                                         (1, 0): evrz_ev[NT - 1],
                                         (0, 1): outp_evac[0],
                                         (1, 1): outp_evac[1]},
                            tile_hook=store_hook, pre_eva=pre_eva2)
            if b + 1 < NB:
                emit_nx_stats(NT - 1)
            carry_ffn2_hilo = {0: (hilo2[(NT - 2, "hi")], hilo2[(NT - 2, "lo")]),
                               1: (hilo2[(NT - 1, "hi")], hilo2[(NT - 1, "lo")])}
            carry_accum = {(0, 0): ffn_evac[(2, NT - 2, 0)],
                           (1, 0): ffn_evac[(2, NT - 2, 1)],
                           (0, 1): ffn_evac[(2, NT - 1, 0)],
                           (1, 1): ffn_evac[(2, NT - 1, 1)]}
            carry_xin = dict(xin_tile)
            carry_pre_eva = pre_eva_nx

        finals = [Ev(f"dXO{k}", S.cnt[f"dXO{k}"], None) for k in range(4)]
        S.op("sync", None, waits=finals)

        def mk_body(name):
            def body(eng):
                hi = {}
                for (e, waits, fn, inc, amt) in S.ops:
                    if e != name:
                        continue
                    for ev in waits:
                        if hi.get(ev.sem, 0) >= ev.val:
                            continue
                        eng.wait_ge(sem[ev.sem], ev.val)
                        hi[ev.sem] = ev.val
                    if fn is not None:
                        ins = fn(eng)
                        if inc is not None:
                            ins.then_inc(sem[inc], amt)
            return body

        block.tensor(mk_body("tensor"))
        block.vector(mk_body("vector"))
        block.scalar(mk_body("scalar"))
        block.gpsimd(mk_body("gpsimd"))
        block.sync(mk_body("sync"))

    nc._sched = S
    return nc


def kernel(L0_attn_hidden_states, L0_attn_W_basis, L0_attn_W_v, L0_attn_b_v,
           L0_attn_W_o, L0_attn_b_o, L0_ffn_weight):
    x = np.ascontiguousarray(np.asarray(L0_attn_hidden_states, dtype=np.float32))
    wv_t = np.ascontiguousarray(np.asarray(L0_attn_W_v, np.float32).T)
    wo_t = np.ascontiguousarray(np.asarray(L0_attn_W_o, np.float32).T)
    wb_t = np.ascontiguousarray(np.asarray(L0_attn_W_basis, np.float32).T)
    w_t = np.ascontiguousarray(np.asarray(L0_ffn_weight, np.float32).T)
    bv = np.ascontiguousarray(np.asarray(L0_attn_b_v, np.float32).reshape(1, C))
    bo = np.ascontiguousarray(np.asarray(L0_attn_b_o, np.float32).reshape(1, C))
    ident = np.eye(128, dtype=np.float32)
    tri = np.triu(np.ones((128, 128), dtype=np.float32))

    nc = build_program()
    in_maps = []
    for c in range(8):
        xc = np.ascontiguousarray(x[c * NB:(c + 1) * NB].reshape(NB * T, C))
        in_maps.append({"x_in": xc, "wv_t": wv_t, "wo_t": wo_t, "wb_t": wb_t,
                        "w_t": w_t, "bv": bv, "bo": bo, "ident": ident,
                        "triu": tri})
    res = run_bass_kernel_spmd(nc, in_maps, core_ids=list(range(8)))
    out = np.concatenate([np.asarray(r["y_out"]).reshape(NB, T, C)
                          for r in res.results], axis=0)
    return out.astype(np.float32, copy=False)


# version 39
# speedup vs baseline: 1.1741x; 1.0152x over previous
import numpy as np
import concourse.bass as bass
import concourse.mybir as mybir
from concourse.bass_utils import run_bass_kernel_spmd

F32 = mybir.dt.float32
BF16 = mybir.dt.bfloat16
AF = mybir.ActivationFunctionType
ALU = mybir.AluOpType
AX = mybir.AxisListType

NB = 4
T = 2048
NT = 16
C = 1024
NCH = 8
R = 64
EPS = 1e-5


class Ev:
    __slots__ = ("sem", "val", "eng")

    def __init__(self, sem, val, eng):
        self.sem, self.val, self.eng = sem, val, eng


class Sched:
    def __init__(self):
        self.ops = []
        self.cnt = {}

    def op(self, eng, fn, waits=(), inc=None, amt=1):
        w = tuple(e for e in waits if e is not None)
        ev = None
        if inc is not None:
            self.cnt[inc] = self.cnt.get(inc, 0) + amt
            ev = Ev(inc, self.cnt[inc], eng if amt == 1 else None)
        self.ops.append((eng, w, fn, inc, amt))
        return ev


def build_program():
    nc = bass.Bass("TRN2", target_bir_lowering=False)

    x_in = nc.dram_tensor("x_in", [NB * T, C], F32, kind="ExternalInput").ap()
    wv_t = nc.dram_tensor("wv_t", [C, C], F32, kind="ExternalInput").ap()
    wo_t = nc.dram_tensor("wo_t", [C, C], F32, kind="ExternalInput").ap()
    wb_t = nc.dram_tensor("wb_t", [C, R], F32, kind="ExternalInput").ap()
    w_t = nc.dram_tensor("w_t", [C, C], F32, kind="ExternalInput").ap()
    bv_h = nc.dram_tensor("bv", [1, C], F32, kind="ExternalInput").ap()
    bo_h = nc.dram_tensor("bo", [1, C], F32, kind="ExternalInput").ap()
    id_h = nc.dram_tensor("ident", [128, 128], F32, kind="ExternalInput").ap()
    tri_h = nc.dram_tensor("triu", [128, 128], F32, kind="ExternalInput").ap()
    y_out = nc.dram_tensor("y_out", [NB * T, C], F32, kind="ExternalOutput").ap()
    v_scr = nc.dram_tensor("v_scr", [T, 2 * C], BF16, kind="Internal").ap()
    wvl_scr = nc.dram_tensor("wvl_scr", [128, 2 * NCH * C], BF16, kind="Internal").ap()
    wol_scr = nc.dram_tensor("wol_scr", [128, 2 * NCH * C], BF16, kind="Internal").ap()

    S = Sched()

    from contextlib import ExitStack
    with ExitStack() as _st:
        x_sb = _st.enter_context(nc.sbuf_tensor("x_sb", [128, NT * C], F32))
        swt = _st.enter_context(nc.sbuf_tensor("swt", [128, NCH * C], BF16))
        warena = _st.enter_context(nc.sbuf_tensor("warena", [128, 2 * NCH * C], BF16))
        wbth = _st.enter_context(nc.sbuf_tensor("wbth", [128, NCH * R], BF16))
        wbtl = _st.enter_context(nc.sbuf_tensor("wbtl", [128, NCH * R], BF16))
        bvb = _st.enter_context(nc.sbuf_tensor("bvb", [128, C], F32))
        bob = _st.enter_context(nc.sbuf_tensor("bob", [128, C], F32))
        ident = _st.enter_context(nc.sbuf_tensor("ident_sb", [128, 128], F32))
        triu = _st.enter_context(nc.sbuf_tensor("triu_sb", [128, 128], F32))
        bvrow = _st.enter_context(nc.sbuf_tensor("bvrow", [1, C], F32))
        borow = _st.enter_context(nc.sbuf_tensor("borow", [1, C], F32))
        onec = _st.enter_context(nc.sbuf_tensor("onec", [128, 1], F32))
        onecb = _st.enter_context(nc.sbuf_tensor("onecb", [128, 1], BF16))
        oner = _st.enter_context(nc.sbuf_tensor("oner", [1, 128], F32))
        sums = _st.enter_context(nc.sbuf_tensor("sums", [128, 2 * NT], F32))
        sqs = _st.enter_context(nc.sbuf_tensor("sqs", [128, 2 * NT], F32))
        mvec = _st.enter_context(nc.sbuf_tensor("mvec", [128, 2 * NT], F32))
        msq = _st.enter_context(nc.sbuf_tensor("msq", [128, 2 * NT], F32))
        varv = _st.enter_context(nc.sbuf_tensor("varv", [128, 2 * NT], F32))
        sqv = _st.enter_context(nc.sbuf_tensor("sqv", [128, 2 * NT], F32))
        rstd = _st.enter_context(nc.sbuf_tensor("rstd", [128, 2 * NT], F32))
        dump = _st.enter_context(nc.sbuf_tensor("dump", [128, C], F32))
        xh32 = _st.enter_context(nc.sbuf_tensor("xh32", [128, 2 * C], F32))
        xhi = _st.enter_context(nc.sbuf_tensor("xhi", [128, 2 * C], BF16))
        xlo = _st.enter_context(nc.sbuf_tensor("xlo", [128, 2 * C], BF16))
        lath = _st.enter_context(nc.sbuf_tensor("lath", [64, T], BF16))
        latl = _st.enter_context(nc.sbuf_tensor("latl", [64, T], BF16))
        mx4 = _st.enter_context(nc.sbuf_tensor("mx4", [1, 4], F32))
        mxrow = _st.enter_context(nc.sbuf_tensor("mxrow", [1, 1], F32))
        cbias = _st.enter_context(nc.sbuf_tensor("cbias", [128, 1], F32))
        e_sb = _st.enter_context(nc.sbuf_tensor("e_sb", [128, 3 * 128], F32))
        ehb = _st.enter_context(nc.sbuf_tensor("ehb", [128, 3 * 128], BF16))
        elb = _st.enter_context(nc.sbuf_tensor("elb", [128, 3 * 128], BF16))
        vwl = _st.enter_context(nc.sbuf_tensor("vwl", [128, 2 * 2 * C], BF16))
        vr = _st.enter_context(nc.sbuf_tensor("vr", [128, 5 * 2 * C], BF16))
        y32 = _st.enter_context(nc.sbuf_tensor("y32", [128, C], F32))
        yfml = _st.enter_context(nc.sbuf_tensor("yfml", [128, 2 * C], BF16))
        rz = _st.enter_context(nc.sbuf_tensor("rz", [128, 2], F32))
        xsp = _st.enter_context(nc.sbuf_tensor("xsp", [128, C], F32))
        tmpo = _st.enter_context(nc.sbuf_tensor("tmpo", [128, C], F32))
        psB = _st.enter_context(nc.psum_tensor("psB", [128, 512], F32))
        psC = _st.enter_context(nc.psum_tensor("psC", [128, 512], F32))
        psD = _st.enter_context(nc.psum_tensor("psD", [128, 1024], F32))
        psD2 = _st.enter_context(nc.psum_tensor("psD2", [128, 1024], F32))
        psE = _st.enter_context(nc.psum_tensor("psE", [128, 512], F32))
        psF = _st.enter_context(nc.psum_tensor("psF", [128, 512], F32))
        bank = lambda i: psD if i % 2 == 0 else psD2
        sem = {}
        for nm in (["T", "V", "S", "G", "dWG", "dWV", "dWO"]
                   + [f"dIN{i}" for i in range(4)]
                   + [f"dVO{i}" for i in range(2)]
                   + [f"dXI{i}" for i in range(4)]
                   + [f"dXO{i}" for i in range(4)]
                   + [f"dVG{i}" for i in range(4)]
                   + [f"dSC{i}" for i in range(2)]):
            sem[nm] = _st.enter_context(nc.semaphore("sem_" + nm))
        block = _st.enter_context(nc.Block())

        for dst, srch in ((triu, tri_h), (bvrow, bv_h),
                          (borow, bo_h)):
            S.op("gpsimd",
                 lambda e, d=dst, s=srch: e.dma_start(d[:, :], s[:, :]),
                 inc="dWG", amt=16)
        ev_consts_all = S.op("gpsimd",
                             lambda e: e.dma_start(ident[:, :], id_h[:, :]),
                             inc="dWG", amt=16)
        ev_ones = S.op("vector", lambda e: e.memset(onec[:, :], 1.0), inc="V")
        ev_onesb = S.op("vector", lambda e: e.memset(onecb[:, :], 1.0), inc="V")
        ev_oner = S.op("vector", lambda e: e.memset(oner[:, :], 1.0), inc="V")

        ev_bb = {}
        for bi, (row, tile) in enumerate(((bvrow, bvb), (borow, bob))):
            for h, ps in ((0, psB), (1, psC)):
                evm = S.op("tensor",
                           lambda e, r=row, h=h, ps=ps: e.matmul(
                               ps[:, 0:512], oner[0:1, :], r[0:1, h * 512:(h + 1) * 512],
                               start=True, stop=True),
                           waits=[ev_oner, ev_consts_all] + ([ev_bb.get((bi - 1, h))] if bi else []),
                           inc="T")
                ev_bb[(bi, h)] = S.op("vector",
                                      lambda e, t=tile, h=h, ps=ps: e.tensor_copy(
                                          t[:, h * 512:(h + 1) * 512], ps[:, 0:512]),
                                      waits=[evm], inc="V")

        LO = NCH * C
        stg_bufs = [(tmpo, 0), (xsp, 0), (dump, 0), (y32, 0)]
        stg_free = [None, None, None, None]
        last_dIN = {}

        def stage4(i, src_slice):
            buf, off = stg_bufs[i]
            eng, sm = (("sync", f"dIN{i}") if i < 2
                       else ("scalar", f"dSC{i - 2}"))
            evd = S.op(eng,
                       lambda e, s=src_slice, buf=buf, off=off: e.dma_start(
                           buf[:, off:off + C], s),
                       waits=[stg_free[i]], inc=sm, amt=16)
            if i < 2:
                last_dIN[i] = evd
            return evd

        ev_sign = [None] * NCH
        dma_ev = {}
        for ic in range(4):
            dma_ev[ic] = stage4(ic, w_t[ic * 128:(ic + 1) * 128, :])
        for ic in range(NCH):
            evd = dma_ev[ic]
            buf, off = stg_bufs[ic % 4]
            ev_sign[ic] = S.op("scalar",
                               lambda e, ic=ic, buf=buf, off=off: e.activation(
                                   swt[:, ic * C:(ic + 1) * C],
                                   buf[:, off:off + C], AF.Sign),
                               waits=[evd], inc="S")
            stg_free[ic % 4] = ev_sign[ic]
            if ic + 4 < NCH:
                dma_ev[ic + 4] = stage4(
                    ic, w_t[(ic + 4) * 128:(ic + 5) * 128, :])

        wbev = {}

        def emit_wb():
            evd_wb = S.op("sync",
                          lambda e: e.dma_start(
                              tmpo[:, 0:512].rearrange("p (ic r) -> p ic r",
                                                       ic=NCH),
                              wb_t.rearrange("(ic p) r -> p ic r", p=128)),
                          waits=[stg_free[0]], inc="dIN0", amt=16)
            last_dIN[0] = evd_wb
            wbev["h"] = S.op("scalar",
                             lambda e: e.activation(wbth[:, :], tmpo[:, 0:512],
                                                    AF.Copy),
                             waits=[evd_wb], inc="S")
            wbev["l"] = S.op("vector",
                             lambda e: e.tensor_sub(wbtl[:, :], tmpo[:, 0:512],
                                                    wbth[:, :]),
                             waits=[wbev["h"]], inc="V")
            stg_free[0] = wbev["l"]

        init_free = [None, None]
        ev_spill = {}
        wv_ready = [None]

        def emit_wchunk(j):
            if j == 0:
                init_free[0], init_free[1] = stg_free[0], stg_free[1]
            w_src, ic = (wo_t, j) if j < NCH else (wv_t, j - NCH)
            slot = j % 2
            buf, off = stg_bufs[slot]
            evd = S.op("sync",
                       lambda e, s=w_src[ic * 128:(ic + 1) * 128, :],
                       buf=buf, off=off: e.dma_start(
                           buf[:, off:off + C], s),
                       waits=[init_free[slot]], inc=f"dIN{slot}", amt=16)
            last_dIN[slot] = evd
            evh = S.op("scalar",
                       lambda e, ic=ic, buf=buf, off=off: e.activation(
                           warena[:, ic * C:(ic + 1) * C],
                           buf[:, off:off + C], AF.Copy),
                       waits=[evd, ev_spill.get("wo")], inc="S")
            evl = S.op("vector",
                       lambda e, ic=ic, buf=buf, off=off: e.tensor_sub(
                           warena[:, LO + ic * C:LO + (ic + 1) * C],
                           buf[:, off:off + C],
                           warena[:, ic * C:(ic + 1) * C]),
                       waits=[evh], inc="V")
            init_free[slot] = evl
            if j == NCH - 1:
                ev_spill["wo"] = S.op("sync",
                                      lambda e: e.dma_start(wol_scr[:, :],
                                                            warena[:, :]),
                                      waits=[evl], inc="dWV", amt=16)
            elif j == 2 * NCH - 1:
                wv_ready[0] = evl
                ev_spill["wv"] = S.op("sync",
                                      lambda e: e.dma_start(wvl_scr[:, :],
                                                            warena[:, :]),
                                      waits=[evl], inc="dWV", amt=16)

        ev_xout = {}
        ev_wvt = [None, None]
        ev_wot = None
        ev_outp_last = None
        carry_ffn2_hilo = {0: (ev_consts_all,), 1: (ev_consts_all,)}
        carry_accum = {(0, 0): None, (1, 0): None,
                       (0, 1): ev_bb[(1, 0)], (1, 1): ev_bb[(1, 1)]}
        carry_evrz15 = None
        carry_xin = {}
        carry_flush = None
        carry_pre_eva = {}
        x_hbm = x_in.rearrange("(k t p) c -> k p t c", p=128, t=4)
        y_hbm = y_out.rearrange("(k t p) c -> k p t c", p=128, t=4)

        def sb4(k):
            return x_sb[:, k * 4 * C:(k + 1) * 4 * C].rearrange(
                "p (t c) -> p t c", t=4)

        evsq_rd_done = [stg_free[2]]
        tr_ev = [None, None]

        def emit_stat_tile(po, ti, gates, src=None):
            sv = src if src is not None else x_sb[:, ti * C:(ti + 1) * C]
            evsum = S.op("vector",
                         lambda e, po=po, ti=ti, sv=sv: e.reduce_sum(
                             sums[:, po + ti:po + ti + 1], sv, axis=AX.X),
                         waits=gates, inc="V")
            evsq = S.op("scalar",
                        lambda e, ti=ti, sv=sv: e.activation(
                            dump[:, :], sv, AF.Square),
                        waits=list(gates) + [evsq_rd_done[0]], inc="S")
            evsq2 = S.op("vector",
                         lambda e, po=po, ti=ti: e.reduce_sum(
                             sqs[:, po + ti:po + ti + 1], dump[:, :], axis=AX.X),
                         waits=[evsq], inc="V")
            evsq_rd_done[0] = evsq2
            return [evsum, evsq2]

        def emit_finish(po, g, stat_evs):
            c0 = po + 4 * g
            sl = slice(c0, c0 + 4)
            evm0 = S.op("vector",
                        lambda e, sl=sl: e.tensor_scalar_mul(
                            mvec[:, sl], sums[:, sl], 1.0 / C),
                        waits=stat_evs, inc="V")
            evm1 = S.op("vector",
                        lambda e, sl=sl: e.tensor_mul(msq[:, sl], mvec[:, sl],
                                                      mvec[:, sl]),
                        waits=[evm0], inc="V")
            evv = S.op("vector",
                       lambda e, sl=sl: e.tensor_scalar(
                           varv[:, sl], sqs[:, sl], 1.0 / C, None, ALU.mult),
                       waits=stat_evs, inc="V")
            evv2 = S.op("vector",
                        lambda e, sl=sl: e.tensor_sub(varv[:, sl], varv[:, sl],
                                                      msq[:, sl]),
                        waits=[evm1, evv], inc="V")
            evv3 = S.op("vector",
                        lambda e, sl=sl: e.tensor_scalar_add(varv[:, sl],
                                                             varv[:, sl], EPS),
                        waits=[evv2], inc="V")
            evsq = S.op("scalar",
                        lambda e, sl=sl: e.activation(sqv[:, sl], varv[:, sl],
                                                      AF.Sqrt),
                        waits=[evv3], inc="S")
            evr = S.op("vector",
                       lambda e, sl=sl: e.reciprocal(rstd[:, sl], sqv[:, sl]),
                       waits=[evsq], inc="V")
            return evr

        ev_xin0 = {}
        for ti in range(NT):
            g, t = ti // 4, ti % 4
            ev_xin0[ti] = S.op("gpsimd",
                               lambda e, g=g, t=t: e.dma_start(
                                   x_sb[:, (4 * g + t) * C:
                                        (4 * g + t + 1) * C],
                                   x_hbm[g, :, t, :]),
                               waits=[ev_xin0.get(ti - 4)],
                               inc=f"dXI{ti % 4}", amt=16)
        finish_ffn1_b0 = {}
        b0_sevs = {}

        def emit_b0_tile(tj):
            g = tj // 4
            sevs = b0_sevs.setdefault(g, [])
            sevs += emit_stat_tile(0, tj, [ev_xin0[tj]])
            if tj % 4 == 3:
                finish_ffn1_b0[g] = emit_finish(0, g, sevs)

        for tj in range(8):
            emit_b0_tile(tj)
        carry_xin = dict(ev_xin0)

        for b in range(NB):
            finish_ffn1 = finish_ffn1_b0 if b == 0 else carry_finish_ffn1

            def ln_apply(ti, po, gate):
                sl = (ti % 2) * C
                war = [tr_ev[(ti - 2) % 2]] if tr_ev[(ti - 2) % 2] else []
                return S.op("vector",
                            lambda e, ti=ti, sl=sl, po=po: e.tensor_scalar(
                                xh32[:, sl:sl + C], x_sb[:, ti * C:(ti + 1) * C],
                                mvec[:, po + ti:po + ti + 1],
                                rstd[:, po + ti:po + ti + 1],
                                ALU.subtract, ALU.mult),
                            waits=[gate] + war,
                            inc="V")

            def ln_transpose(ti, eva, tr_free_evs):
                sl = (ti % 2) * C

                def fn(e, sl=sl, bk=bank(ti)):
                    last = None
                    for ic in range(NCH):
                        last = e.transpose(bk[:, ic * 128:(ic + 1) * 128],
                                           xh32[:, sl + ic * 128:sl + (ic + 1) * 128],
                                           ident[:, :])
                    return last
                evt = S.op("tensor", fn, waits=[eva] + tr_free_evs, inc="T")
                tr_ev[ti % 2] = evt
                return evt

            def run_ffn(sub, po, finish_evs, tr_carry, accum_carry,
                        tile_hook=None, inject_at2=None, pre_eva=None):
                evac_ev = dict(accum_carry)
                hilo_ev = {}
                mm_ev = {}
                trs = {}
                eva_d = {}

                def emit_eva(ti):
                    if ti >= NT:
                        return
                    if pre_eva and ti in pre_eva:
                        eva_d[ti] = pre_eva[ti]
                        return
                    eva_d[ti] = ln_apply(ti, po, finish_evs[ti // 4])

                def emit_tr(ti):
                    if ti >= NT:
                        return
                    free = ([hilo_ev[(ti - 2, "hi")], hilo_ev[(ti - 2, "lo")]]
                            if ti >= 2 else list(tr_carry.get(ti % 2, ())))
                    trs[ti] = ln_transpose(ti, eva_d[ti],
                                           [f for f in free if f])
                def emit_hilo(ti):
                    if ti >= NT:
                        return
                    evt = trs[ti]
                    sl = (ti % 2) * C
                    bk = bank(ti)
                    war = [mm_ev[(ti - 2, 1)]] if (ti - 2, 1) in mm_ev else []
                    ehi = S.op("scalar",
                               lambda e, sl=sl, bk=bk: e.activation(
                                   xhi[:, sl:sl + C], bk[:, :], AF.Copy),
                               waits=[evt] + war, inc="S")
                    elo = S.op("vector",
                               lambda e, sl=sl, bk=bk: e.tensor_sub(
                                   xlo[:, sl:sl + C], bk[:, :],
                                   xhi[:, sl:sl + C]),
                               waits=[evt, ehi] + war, inc="V")
                    hilo_ev[(ti, "hi")] = ehi
                    hilo_ev[(ti, "lo")] = elo
                emit_eva(0)
                emit_eva(1)
                emit_tr(0)
                emit_eva(2)
                emit_tr(1)
                emit_hilo(0)
                for ti in range(NT):
                    if ti == 6 and inject_at2 is not None:
                        inject_at2()
                    emit_eva(ti + 3)
                    emit_tr(ti + 2)
                    emit_hilo(ti + 1)
                    ehi = hilo_ev[(ti, "hi")]
                    elo = hilo_ev[(ti, "lo")]
                    sl = (ti % 2) * C
                    par = ti % 2
                    for h, ps in ((0, psE if par == 0 else psB),
                                  (1, psF if par == 0 else psC)):
                        def fn(e, sl=sl, h=h, ps=ps):
                            last = None
                            for j, src in enumerate((xhi, xlo)):
                                for ic in range(NCH):
                                    last = e.matmul(
                                        ps[:, 0:512],
                                        src[:, sl + ic * 128:sl + (ic + 1) * 128],
                                        swt[:, ic * C + h * 512:ic * C + (h + 1) * 512],
                                        start=(j == 0 and ic == 0),
                                        stop=(j == 1 and ic == NCH - 1))
                            return last
                        evm = S.op("tensor", fn,
                                   waits=[ehi, elo, evac_ev[(h, par)],
                                          ev_sign[7]], inc="T")
                        mm_ev[(ti, h)] = evm
                        evac_ev[(h, par)] = S.op(
                            "vector",
                            lambda e, ti=ti, h=h, ps=ps:
                            e.scalar_tensor_tensor(
                                x_sb[:, ti * C + h * 512:ti * C + (h + 1) * 512],
                                ps[:, 0:512], 0.5,
                                x_sb[:, ti * C + h * 512:ti * C + (h + 1) * 512],
                                ALU.mult, ALU.add),
                            waits=[evm], inc="V")
                        ffn_evac[(sub, ti, h)] = evac_ev[(h, par)]
                    if tile_hook is not None:
                        tile_hook(ti)
                return hilo_ev

            ffn_evac = {}

            po_f1 = (b % 2) * NT
            po_at = ((b + 1) % 2) * NT
            po_f2 = (b % 2) * NT
            finish_attn = {}

            at_stats = {}

            def attn_hook(ti):
                if b == 0:
                    if ti == 0:
                        emit_wb()
                    if ti < 8:
                        emit_b0_tile(8 + ti)
                sevs = at_stats.setdefault(ti // 4, [])
                sevs += emit_stat_tile(
                    po_at, ti, [ffn_evac[(0, ti, 0)], ffn_evac[(0, ti, 1)]])
                if ti % 4 == 3:
                    finish_attn[ti // 4] = emit_finish(po_at, ti // 4, sevs)
                if b == 0:
                    emit_wchunk(ti)

            hilo1 = run_ffn(0, po_f1, finish_ffn1,
                            tr_carry=carry_ffn2_hilo, accum_carry=carry_accum,
                            tile_hook=attn_hook, inject_at2=carry_flush,
                            pre_eva=carry_pre_eva)

            if b == 0:
                ev_wvt = [None, wv_ready[0]]
            else:
                ev_wvt = []
                for k in range(2):
                    ev_wvt.append(S.op("sync",
                                       lambda e, k=k: e.dma_start(
                                           warena[:, k * LO:(k + 1) * LO],
                                           wvl_scr[:, k * LO:(k + 1) * LO]),
                                       waits=[ev_outp_last, ev_spill["wv"]],
                                       inc="dWV", amt=16))

            lat_ev = None
            xfm_ev = {}
            vmm_done = {}
            vh_ev = {}
            vl_ev = {}
            vout_ev = {}
            atr = {}

            aeva_d = {}

            def emit_aeva(ti):
                if ti >= NT:
                    return
                aeva_d[ti] = ln_apply(ti, po_at, finish_attn[ti // 4])

            def emit_atr(ti):
                if ti >= NT:
                    return
                free = ([xfm_ev[ti - 2]] if ti >= 2 else
                        [hilo1.get((14 + ti, "hi")), hilo1.get((14 + ti, "lo"))])
                atr[ti] = ln_transpose(ti, aeva_d[ti], [f for f in free if f])
            evxh_d = {}

            def emit_bx(ti):
                if ti >= NT:
                    return
                evt = atr[ti]
                sl = (ti % 2) * C
                bk = bank(ti)
                free = [vmm_done[ti - 2]] if ti - 2 in vmm_done else []
                evxh = S.op("scalar",
                            lambda e, sl=sl, bk=bk: e.activation(
                                xhi[:, sl:sl + C], bk[:, :], AF.Copy),
                            waits=[evt] + free, inc="S")
                evxl = S.op("vector",
                            lambda e, sl=sl, bk=bk: e.tensor_sub(
                                xlo[:, sl:sl + C], bk[:, :],
                                xhi[:, sl:sl + C]),
                            waits=[evxh], inc="V")
                evxh_d[ti] = evxh
                xfm_ev[ti] = evxl
            emit_aeva(0)
            emit_aeva(1)
            emit_atr(0)
            emit_aeva(2)
            emit_atr(1)
            emit_bx(0)
            for ti in range(NT):
                emit_aeva(ti + 3)
                emit_atr(ti + 2)
                emit_bx(ti + 1)
                evxh = evxh_d[ti]
                evxl = xfm_ev[ti]
                sl = (ti % 2) * C

                def fn_lat(e, sl=sl, ti=ti):
                    last = None
                    for ic in range(NCH):
                        for j, (wl, src) in enumerate(
                                ((wbth, xhi), (wbtl, xhi), (wbth, xlo))):
                            last = e.matmul(
                                psF[0:64, 0:128],
                                wl[:, ic * R:(ic + 1) * R],
                                src[:, sl + ic * 128:sl + (ic + 1) * 128],
                                start=(ic == 0 and j == 0),
                                stop=(ic == NCH - 1 and j == 2))
                    return last
                evlm = S.op("tensor", fn_lat,
                            waits=[evxh, evxl, wbev["h"], wbev["l"],
                                   lat_ev if ti else
                                   ffn_evac[(0, NT - 2, 1)]], inc="T")
                lh_ev = S.op("scalar",
                             lambda e, ti=ti: e.activation(
                                 lath[0:64, ti * 128:(ti + 1) * 128],
                                 psF[0:64, 0:128], AF.Copy),
                             waits=[evlm], inc="S")
                lat_ev = S.op("vector",
                              lambda e, ti=ti: e.tensor_sub(
                                  latl[0:64, ti * 128:(ti + 1) * 128],
                                  psF[0:64, 0:128],
                                  lath[0:64, ti * 128:(ti + 1) * 128]),
                              waits=[lh_ev], inc="V")

                sl3 = (ti % 2) * 2 * C
                for h, ps in ((0, psB), (1, psC)):
                    def fn_v(e, sl=sl, h=h, ps=ps):
                        last = None
                        for ic in range(NCH):
                            for j, (src, woff) in enumerate(
                                    ((xhi, 0), (xhi, LO), (xlo, 0))):
                                last = e.matmul(
                                    ps[:, 0:512],
                                    src[:, sl + ic * 128:sl + (ic + 1) * 128],
                                    warena[:, woff + ic * C + h * 512:
                                           woff + ic * C + (h + 1) * 512],
                                    start=(ic == 0 and j == 0),
                                    stop=(ic == NCH - 1 and j == 2))
                        return last
                    evm = S.op("tensor", fn_v,
                               waits=[evxh, evxl, ev_wvt[1],
                                      vl_ev.get((ti - 1, h)),
                                      ffn_evac[(0, NT - 1, h)]],
                               inc="T")
                    vmm_done[ti] = evm
                    vh_ev[(ti, h)] = S.op(
                        "scalar",
                        lambda e, sl3=sl3, h=h, ps=ps: e.activation(
                            vwl[:, sl3 + h * 512:sl3 + (h + 1) * 512],
                            ps[:, 0:512], AF.Copy),
                        waits=[evm, vout_ev.get(ti - 2)], inc="S")
                    vl_ev[(ti, h)] = S.op(
                        "vector",
                        lambda e, sl3=sl3, h=h, ps=ps: e.tensor_sub(
                            vwl[:, sl3 + C + h * 512:sl3 + C + (h + 1) * 512],
                            ps[:, 0:512],
                            vwl[:, sl3 + h * 512:sl3 + (h + 1) * 512]),
                        waits=[vh_ev[(ti, h)]], inc="V")
                vout_ev[ti] = S.op("scalar",
                                   lambda e, ti=ti, sl3=sl3: e.dma_start(
                                       v_scr[ti * 128:(ti + 1) * 128, :],
                                       vwl[:, sl3:sl3 + 2 * C]),
                                   waits=[vl_ev[(ti, 0)], vl_ev[(ti, 1)]],
                                   inc=f"dVO{ti % 2}", amt=16)

            n2_banks = [psB, psC, psF, psD]
            n2_ev = []
            prev_evac = None
            for j in range(4):
                eva2 = S.op("vector",
                            lambda e, j=j: e.tensor_add(
                                dump[0:64, 512:1024],
                                lath[0:64, j * 512:(j + 1) * 512],
                                latl[0:64, j * 512:(j + 1) * 512]),
                            waits=[lat_ev, prev_evac, evsq_rd_done[0]],
                            inc="V")
                evq = S.op("scalar",
                           lambda e: e.activation(
                               dump[0:64, 512:1024], dump[0:64, 512:1024],
                               AF.Square),
                           waits=[eva2], inc="S")
                wait_bank = []
                if n2_banks[j] is psB:
                    wait_bank = [vl_ev[(NT - 1, 0)]]
                elif n2_banks[j] is psC:
                    wait_bank = [vl_ev[(NT - 1, 1)]]
                elif n2_banks[j] is psD:
                    wait_bank = [xfm_ev[NT - 2]]
                evm = S.op("tensor",
                           lambda e, j=j: e.matmul(
                               n2_banks[j][0:1, 0:512], onec[0:64, 0:1],
                               dump[0:64, 512:1024], start=True, stop=True),
                           waits=[evq, ev_ones] + wait_bank, inc="T")
                prev_evac = S.op("vector",
                                 lambda e, j=j: e.reduce_max(
                                     mx4[0:1, j:j + 1],
                                     n2_banks[j][0:1, 0:512], axis=AX.X),
                                 waits=[evm], inc="V")
                n2_ev.append(prev_evac)
            evmx = S.op("vector",
                        lambda e: e.reduce_max(mxrow[0:1, 0:1], mx4[0:1, :],
                                               axis=AX.X),
                        waits=n2_ev, inc="V")
            evcb_m = S.op("tensor",
                          lambda e: e.matmul(psE[:, 0:1], oner[0:1, :],
                                             mxrow[0:1, 0:1], start=True, stop=True),
                          waits=[evmx, ev_oner], inc="T")
            evcb = S.op("vector",
                        lambda e: e.tensor_scalar_mul(cbias[:, :], psE[:, 0:1],
                                                      -0.125),
                        waits=[evcb_m], inc="V")

            ev_wot = []
            for k in range(2):
                ev_wot.append(S.op("sync",
                                   lambda e, k=k: e.dma_start(
                                       warena[:, k * LO:(k + 1) * LO],
                                       wol_scr[:, k * LO:(k + 1) * LO]),
                                   waits=[vmm_done[NT - 1]] +
                                         ([ev_spill["wv"]] if b == 0 else []),
                                   inc="dWO", amt=16))

            pairs = [(qi, ki) for qi in range(NT) for ki in range(qi + 1)]
            vin_ev = {}
            ving_ev = {}
            emitted = [0]

            def emit_vin(n):
                while emitted[0] < min(n, len(pairs)):
                    idx = emitted[0]
                    qi, ki = pairs[idx]
                    slot = idx % 5
                    sm = f"dIN{slot}" if slot < 4 else "dVG0"
                    wait_slot = ([yz_ev[idx - 5]] if idx >= 5 else
                                 [last_dIN.get(slot)])
                    vin_ev[idx] = S.op(
                        "sync",
                        lambda e, ki=ki, slot=slot: e.dma_start(
                            vr[:, slot * 2 * C:(slot + 1) * 2 * C],
                            v_scr[ki * 128:(ki + 1) * 128, :]),
                        waits=wait_slot + [vout_ev[ki]],
                        inc=sm, amt=16)
                    emitted[0] += 1

            yz_ev = {}
            exp_ev = {}
            scr = {}
            yfl_ev = {}
            ytr_ev = {}
            evrz_ev = {}
            outp_evac = {0: None, 1: None}
            outp_holder = [None]
            oev_t = {}
            bob_pre = {}
            pend_tail = []
            finish_ffn2 = {}
            f2stats = {}
            emit_vin(5)

            def emit_f2_stats(ti):
                sevs = f2stats.setdefault(ti // 4, [])
                sevs += emit_stat_tile(po_f2, ti,
                                       [oev_t[(ti, 0)], oev_t[(ti, 1)]])
                if ti % 4 == 3:
                    finish_ffn2[ti // 4] = emit_finish(po_f2, ti // 4, sevs)

            def emit_score(idx):
                if idx >= len(pairs):
                    return
                qi, ki = pairs[idx]
                slot = idx % 3
                esl = 128 + slot * 128
                es = slot * 128
                def fn_st(e, qi=qi, ki=ki, esl=esl):
                    last = None
                    for j, (kl, ql) in enumerate(
                            ((lath, lath), (lath, latl), (latl, lath))):
                        last = e.matmul(
                            psE[:, esl:esl + 128],
                            kl[0:64, ki * 128:(ki + 1) * 128],
                            ql[0:64, qi * 128:(qi + 1) * 128],
                            start=(j == 0), stop=(j == 2))
                    return last
                evst = S.op("tensor", fn_st,
                            waits=[lat_ev, evcb, exp_ev.get(idx - 1)], inc="T")
                eve = S.op("scalar",
                           lambda e, esl=esl, es=es: e.activation(
                               e_sb[:, es:es + 128], psE[:, esl:esl + 128],
                               AF.Exp, bias=cbias[:, 0:1], scale=0.125),
                           waits=[evst, evcb, yz_ev.get(idx - 3)], inc="S")
                exp_ev[idx] = eve
                gate = eve
                if ki == qi:
                    gate = S.op("vector",
                                lambda e, es=es: e.tensor_mul(
                                    e_sb[:, es:es + 128], e_sb[:, es:es + 128],
                                    triu[:, :]),
                                waits=[eve, ev_consts_all], inc="V")
                evh = S.op("vector",
                           lambda e, es=es: e.tensor_copy(
                               ehb[:, es:es + 128], e_sb[:, es:es + 128]),
                           waits=[gate], inc="V")
                evl = S.op("vector",
                           lambda e, es=es: e.tensor_sub(
                               elb[:, es:es + 128], e_sb[:, es:es + 128],
                               ehb[:, es:es + 128]),
                           waits=[evh], inc="V")
                scr[idx] = (gate, evh, evl)

            def emit_rz(qi):
                idx_d = qi * (qi + 1) // 2 + qi
                zc = qi % 2
                evrz_ev[qi] = S.op("vector",
                                   lambda e, zc=zc: e.reciprocal(
                                       rz[:, zc:zc + 1], psF[:, 0:1]),
                                   waits=[yz_ev[idx_d]], inc="V")

            def emit_tail_a(qi):
                pd = bank(qi)
                zc = qi % 2
                bob_pre[qi] = S.op("vector",
                                   lambda e, qi=qi: e.tensor_add(
                                       x_sb[:, qi * C:(qi + 1) * C],
                                       x_sb[:, qi * C:(qi + 1) * C],
                                       bob[:, :]),
                                   waits=[aeva_d[qi], ev_bb[(1, 0)],
                                          ev_bb[(1, 1)]],
                                   inc="V")
                y32_ev = S.op("vector",
                              lambda e, pd=pd, zc=zc: e.scalar_tensor_tensor(
                                  y32[:, :], pd[:, :], rz[:, zc:zc + 1],
                                  bvb[:, :], ALU.mult, ALU.add),
                              waits=[evrz_ev[qi], ev_bb[(0, 0)], ev_bb[(0, 1)],
                                     ytr_ev.get(qi - 1)],
                              inc="V")

                def fn_ytr(e, pd=pd):
                    last = None
                    for cc in range(NCH):
                        last = e.transpose(pd[:, cc * 128:(cc + 1) * 128],
                                           y32[:, cc * 128:(cc + 1) * 128],
                                           ident[:, :])
                    return last
                ytr_ev[qi] = S.op("tensor", fn_ytr, waits=[y32_ev], inc="T")

            def emit_tail_b(qi):
                pd = bank(qi)
                evyt = ytr_ev[qi]
                yfh = S.op("scalar",
                           lambda e, pd=pd: e.activation(yfml[:, 0:C],
                                                         pd[:, :], AF.Copy),
                           waits=[evyt], inc="S")
                yfl = S.op("vector",
                           lambda e, pd=pd: e.tensor_sub(
                               yfml[:, C:2 * C], pd[:, :], yfml[:, 0:C]),
                           waits=[yfh], inc="V")
                yfl_ev[qi] = yfl
                for h, ps in ((0, psB), (1, psC)):
                    def fn_op(e, h=h, ps=ps):
                        last = None
                        for cc in range(NCH):
                            for j, (aoff, woff) in enumerate(
                                    ((0, 0), (0, LO), (C, 0))):
                                last = e.matmul(
                                    ps[:, 0:512],
                                    yfml[:, aoff + cc * 128:
                                         aoff + (cc + 1) * 128],
                                    warena[:, woff + cc * C + h * 512:
                                           woff + cc * C + (h + 1) * 512],
                                    start=(cc == 0 and j == 0),
                                    stop=(cc == NCH - 1 and j == 2))
                        return last
                    wb = [yfh, yfl, ev_wot[1], outp_evac[h]]
                    if qi == 0:
                        wb.append(n2_ev[0 if h == 0 else 1])
                    evm = S.op("tensor", fn_op, waits=wb, inc="T")
                    outp_holder[0] = evm
                    evv = S.op("vector",
                               lambda e, qi=qi, h=h, ps=ps: e.tensor_add(
                                   x_sb[:, qi * C + h * 512:
                                        qi * C + (h + 1) * 512],
                                   ps[:, 0:512],
                                   x_sb[:, qi * C + h * 512:
                                        qi * C + (h + 1) * 512]),
                               waits=[evm, bob_pre[qi]], inc="V")
                    outp_evac[h] = evv
                    oev_t[(qi, h)] = evv
                if qi >= 1:
                    emit_f2_stats(qi - 1)

            emit_score(0)
            emit_score(1)
            pre_eva2 = {}
            for idx, (qi, ki) in enumerate(pairs):
                emit_vin(idx + 5)
                while pend_tail and pend_tail[0][0] <= idx:
                    pend_tail.pop(0)[1]()
                if idx == 104:
                    pre_eva2[0] = ln_apply(0, po_f2, finish_ffn2[0])
                    pre_eva2[1] = ln_apply(1, po_f2, finish_ffn2[0])
                emit_score(idx + 2)
                gate, evh, evl = scr[idx]
                es = (idx % 3) * 128
                vslot = idx % 5
                pd = bank(qi)

                def fn_yz(e, qi=qi, ki=ki, vslot=vslot, es=es, pd=pd):
                    st = ki == 0
                    sp = ki == qi
                    base = vslot * 2 * C
                    for j, (ab, voff) in enumerate(((ehb, 0), (ehb, C), (elb, 0))):
                        e.matmul(pd[:, 0:512], ab[:, es:es + 128],
                                 vr[:, base + voff:base + voff + 512],
                                 start=(st and j == 0), stop=(sp and j == 2))
                        e.matmul(pd[:, 512:1024], ab[:, es:es + 128],
                                 vr[:, base + voff + 512:base + voff + 1024],
                                 start=(st and j == 0), stop=(sp and j == 2))
                    return e.matmul(psF[:, 0:1],
                                    e_sb[:, es:es + 128],
                                    onec[:, 0:1], start=st, stop=sp)
                w = [gate, evh, evl, vin_ev[idx]]
                if ki == 0:
                    if qi >= 1:
                        w.append(evrz_ev[qi - 1])
                    if qi >= 2:
                        w.append(yfl_ev[qi - 2])
                    elif qi < 2:
                        w.append(xfm_ev[NT - 2 + qi])
                yz_ev[idx] = S.op("tensor", fn_yz, waits=w, inc="T")
                if ki == qi:
                    emit_rz(qi)
                    pend_tail.append((idx + 2, lambda qi=qi: emit_tail_a(qi)))
                    pend_tail.append((idx + 3, lambda qi=qi: emit_tail_b(qi)))
            for _, fn in pend_tail:
                fn()
            ev_outp_last = outp_holder[0]
            carry_evrz15 = evrz_ev[NT - 1]

            carry_finish_ffn1 = {}
            po_nx = ((b + 1) % 2) * NT
            xin_tile = {}
            nx_stats = {}
            xsp_cp = [None]
            pre_eva_nx = {}

            def emit_nx_stats(ti):
                sevs = nx_stats.setdefault(ti // 4, [])
                sevs += emit_stat_tile(
                    po_nx, ti, [xin_tile[ti]],
                    src=(xsp[:, :] if ti == NT - 1 else None))
                if ti == NT - 1:
                    sevs.append(xsp_cp[0])
                if ti % 4 == 3:
                    carry_finish_ffn1[ti // 4] = emit_finish(
                        po_nx, ti // 4, sevs)

            def store_hook(ti):
                if ti == 0:
                    emit_f2_stats(NT - 1)
                g, t = ti // 4, ti % 4
                ev_xout[(b, ti)] = S.op(
                    "gpsimd",
                    lambda e, b=b, g=g, t=t, ti=ti: e.dma_start(
                        y_hbm[b * 4 + g, :, t, :],
                        x_sb[:, ti * C:(ti + 1) * C]),
                    waits=[ffn_evac[(2, ti, 0)], ffn_evac[(2, ti, 1)]],
                    inc=f"dXO{ti % 4}", amt=16)
                if b + 1 < NB:
                    if ti == NT - 1:
                        xin_tile[ti] = S.op(
                            "gpsimd",
                            lambda e, b=b: e.dma_start(
                                xsp[:, :], x_hbm[(b + 1) * 4 + 3, :, 3, :]),
                            waits=[xin_tile.get(ti - 4,
                                                carry_xin.get(ti + 12)),
                                   finish_ffn1[3]] +
                                  ([init_free[1]] if b == 0 else []),
                            inc=f"dXI{ti % 4}", amt=16)
                        xsp_cp[0] = S.op(
                            "gpsimd",
                            lambda e, ti=ti: e.tensor_copy(
                                x_sb[:, ti * C:(ti + 1) * C], xsp[:, :]),
                            waits=[ev_xout[(b, ti)], xin_tile[ti]], inc="G")
                    else:
                        xin_tile[ti] = S.op(
                            "gpsimd",
                            lambda e, b=b, g=g, t=t, ti=ti: e.dma_start(
                                x_sb[:, ti * C:(ti + 1) * C],
                                x_hbm[(b + 1) * 4 + g, :, t, :]),
                            waits=[ev_xout[(b, ti)],
                                   xin_tile.get(ti - 4,
                                                carry_xin.get(ti + 12))],
                            inc=f"dXI{ti % 4}", amt=16)
                    if ti >= 1:
                        emit_nx_stats(ti - 1)
                    if ti == 14:
                        pre_eva_nx[0] = ln_apply(0, po_nx,
                                                 carry_finish_ffn1[0])
                    elif ti == 15:
                        pre_eva_nx[1] = ln_apply(1, po_nx,
                                                 carry_finish_ffn1[0])

            hilo2 = run_ffn(2, po_f2, finish_ffn2,
                            tr_carry={0: (yfl_ev[NT - 2],),
                                      1: (yfl_ev[NT - 1],)},
                            accum_carry={(0, 0): exp_ev[len(pairs) - 1],
                                         (1, 0): evrz_ev[NT - 1],
                                         (0, 1): outp_evac[0],
                                         (1, 1): outp_evac[1]},
                            tile_hook=store_hook, pre_eva=pre_eva2)
            if b + 1 < NB:
                emit_nx_stats(NT - 1)
            carry_ffn2_hilo = {0: (hilo2[(NT - 2, "hi")], hilo2[(NT - 2, "lo")]),
                               1: (hilo2[(NT - 1, "hi")], hilo2[(NT - 1, "lo")])}
            carry_accum = {(0, 0): ffn_evac[(2, NT - 2, 0)],
                           (1, 0): ffn_evac[(2, NT - 2, 1)],
                           (0, 1): ffn_evac[(2, NT - 1, 0)],
                           (1, 1): ffn_evac[(2, NT - 1, 1)]}
            carry_xin = dict(xin_tile)
            carry_pre_eva = pre_eva_nx

        finals = [Ev(f"dXO{k}", S.cnt[f"dXO{k}"], None) for k in range(4)]
        S.op("sync", None, waits=finals)

        def mk_body(name):
            def body(eng):
                hi = {}
                for (e, waits, fn, inc, amt) in S.ops:
                    if e != name:
                        continue
                    for ev in waits:
                        if hi.get(ev.sem, 0) >= ev.val:
                            continue
                        eng.wait_ge(sem[ev.sem], ev.val)
                        hi[ev.sem] = ev.val
                    if fn is not None:
                        ins = fn(eng)
                        if inc is not None:
                            ins.then_inc(sem[inc], amt)
            return body

        block.tensor(mk_body("tensor"))
        block.vector(mk_body("vector"))
        block.scalar(mk_body("scalar"))
        block.gpsimd(mk_body("gpsimd"))
        block.sync(mk_body("sync"))

    nc._sched = S
    return nc


def kernel(L0_attn_hidden_states, L0_attn_W_basis, L0_attn_W_v, L0_attn_b_v,
           L0_attn_W_o, L0_attn_b_o, L0_ffn_weight):
    x = np.ascontiguousarray(np.asarray(L0_attn_hidden_states, dtype=np.float32))
    wv_t = np.ascontiguousarray(np.asarray(L0_attn_W_v, np.float32).T)
    wo_t = np.ascontiguousarray(np.asarray(L0_attn_W_o, np.float32).T)
    wb_t = np.ascontiguousarray(np.asarray(L0_attn_W_basis, np.float32).T)
    w_t = np.ascontiguousarray(np.asarray(L0_ffn_weight, np.float32).T)
    bv = np.ascontiguousarray(np.asarray(L0_attn_b_v, np.float32).reshape(1, C))
    bo = np.ascontiguousarray(np.asarray(L0_attn_b_o, np.float32).reshape(1, C))
    ident = np.eye(128, dtype=np.float32)
    tri = np.triu(np.ones((128, 128), dtype=np.float32))

    nc = build_program()
    in_maps = []
    for c in range(8):
        xc = np.ascontiguousarray(x[c * NB:(c + 1) * NB].reshape(NB * T, C))
        in_maps.append({"x_in": xc, "wv_t": wv_t, "wo_t": wo_t, "wb_t": wb_t,
                        "w_t": w_t, "bv": bv, "bo": bo, "ident": ident,
                        "triu": tri})
    res = run_bass_kernel_spmd(nc, in_maps, core_ids=list(range(8)))
    out = np.concatenate([np.asarray(r["y_out"]).reshape(NB, T, C)
                          for r in res.results], axis=0)
    return out.astype(np.float32, copy=False)


# version 40 (final)
# speedup vs baseline: 1.1975x; 1.0200x over previous
import numpy as np
import concourse.bass as bass
import concourse.mybir as mybir
from concourse.bass_utils import run_bass_kernel_spmd

F32 = mybir.dt.float32
BF16 = mybir.dt.bfloat16
AF = mybir.ActivationFunctionType
ALU = mybir.AluOpType
AX = mybir.AxisListType

NB = 4
T = 2048
NT = 16
C = 1024
NCH = 8
R = 64
EPS = 1e-5


class Ev:
    __slots__ = ("sem", "val", "eng")

    def __init__(self, sem, val, eng):
        self.sem, self.val, self.eng = sem, val, eng


class Sched:
    def __init__(self):
        self.ops = []
        self.cnt = {}

    def op(self, eng, fn, waits=(), inc=None, amt=1):
        w = tuple(e for e in waits if e is not None)
        ev = None
        if inc is not None:
            self.cnt[inc] = self.cnt.get(inc, 0) + amt
            ev = Ev(inc, self.cnt[inc], eng if amt == 1 else None)
        self.ops.append((eng, w, fn, inc, amt))
        return ev


def build_program():
    nc = bass.Bass("TRN2", target_bir_lowering=False)

    x_in = nc.dram_tensor("x_in", [NB * T, C], F32, kind="ExternalInput").ap()
    wv_t = nc.dram_tensor("wv_t", [C, C], F32, kind="ExternalInput").ap()
    wo_t = nc.dram_tensor("wo_t", [C, C], F32, kind="ExternalInput").ap()
    wb_t = nc.dram_tensor("wb_t", [C, R], F32, kind="ExternalInput").ap()
    w_t = nc.dram_tensor("w_t", [C, C], F32, kind="ExternalInput").ap()
    bv_h = nc.dram_tensor("bv", [1, C], F32, kind="ExternalInput").ap()
    bo_h = nc.dram_tensor("bo", [1, C], F32, kind="ExternalInput").ap()
    id_h = nc.dram_tensor("ident", [128, 128], F32, kind="ExternalInput").ap()
    tri_h = nc.dram_tensor("triu", [128, 128], F32, kind="ExternalInput").ap()
    y_out = nc.dram_tensor("y_out", [NB * T, C], F32, kind="ExternalOutput").ap()
    v_scr = nc.dram_tensor("v_scr", [T, 2 * C], BF16, kind="Internal").ap()
    wvl_scr = nc.dram_tensor("wvl_scr", [128, 2 * NCH * C], BF16, kind="Internal").ap()
    wol_scr = nc.dram_tensor("wol_scr", [128, 2 * NCH * C], BF16, kind="Internal").ap()

    S = Sched()

    from contextlib import ExitStack
    with ExitStack() as _st:
        x_sb = _st.enter_context(nc.sbuf_tensor("x_sb", [128, NT * C], F32))
        swt = _st.enter_context(nc.sbuf_tensor("swt", [128, NCH * C], BF16))
        warena = _st.enter_context(nc.sbuf_tensor("warena", [128, 2 * NCH * C], BF16))
        wbth = _st.enter_context(nc.sbuf_tensor("wbth", [128, NCH * R], BF16))
        wbtl = _st.enter_context(nc.sbuf_tensor("wbtl", [128, NCH * R], BF16))
        bvb = _st.enter_context(nc.sbuf_tensor("bvb", [128, C], F32))
        bob = _st.enter_context(nc.sbuf_tensor("bob", [128, C], F32))
        ident = _st.enter_context(nc.sbuf_tensor("ident_sb", [128, 128], F32))
        triu = _st.enter_context(nc.sbuf_tensor("triu_sb", [128, 128], F32))
        bvrow = _st.enter_context(nc.sbuf_tensor("bvrow", [1, C], F32))
        borow = _st.enter_context(nc.sbuf_tensor("borow", [1, C], F32))
        onec = _st.enter_context(nc.sbuf_tensor("onec", [128, 1], F32))
        onecb = _st.enter_context(nc.sbuf_tensor("onecb", [128, 1], BF16))
        oner = _st.enter_context(nc.sbuf_tensor("oner", [1, 128], F32))
        sums = _st.enter_context(nc.sbuf_tensor("sums", [128, 2 * NT], F32))
        sqs = _st.enter_context(nc.sbuf_tensor("sqs", [128, 2 * NT], F32))
        mvec = _st.enter_context(nc.sbuf_tensor("mvec", [128, 2 * NT], F32))
        msq = _st.enter_context(nc.sbuf_tensor("msq", [128, 2 * NT], F32))
        varv = _st.enter_context(nc.sbuf_tensor("varv", [128, 2 * NT], F32))
        sqv = _st.enter_context(nc.sbuf_tensor("sqv", [128, 2 * NT], F32))
        rstd = _st.enter_context(nc.sbuf_tensor("rstd", [128, 2 * NT], F32))
        dump = _st.enter_context(nc.sbuf_tensor("dump", [128, C], F32))
        xh32 = _st.enter_context(nc.sbuf_tensor("xh32", [128, 2 * C], F32))
        xhi = _st.enter_context(nc.sbuf_tensor("xhi", [128, 2 * C], BF16))
        xlo = _st.enter_context(nc.sbuf_tensor("xlo", [128, 2 * C], BF16))
        lath = _st.enter_context(nc.sbuf_tensor("lath", [64, T], BF16))
        latl = _st.enter_context(nc.sbuf_tensor("latl", [64, T], BF16))
        mx4 = _st.enter_context(nc.sbuf_tensor("mx4", [1, 4], F32))
        mxrow = _st.enter_context(nc.sbuf_tensor("mxrow", [1, 1], F32))
        cbias = _st.enter_context(nc.sbuf_tensor("cbias", [128, 1], F32))
        e_sb = _st.enter_context(nc.sbuf_tensor("e_sb", [128, 3 * 128], F32))
        ehb = _st.enter_context(nc.sbuf_tensor("ehb", [128, 3 * 128], BF16))
        elb = _st.enter_context(nc.sbuf_tensor("elb", [128, 3 * 128], BF16))
        vwl = _st.enter_context(nc.sbuf_tensor("vwl", [128, 2 * 2 * C], BF16))
        vr = _st.enter_context(nc.sbuf_tensor("vr", [128, 5 * 2 * C], BF16))
        y32 = _st.enter_context(nc.sbuf_tensor("y32", [128, C], F32))
        yfml = _st.enter_context(nc.sbuf_tensor("yfml", [128, 2 * C], BF16))
        rz = _st.enter_context(nc.sbuf_tensor("rz", [128, 2], F32))
        xsp = _st.enter_context(nc.sbuf_tensor("xsp", [128, C], F32))
        tmpo = _st.enter_context(nc.sbuf_tensor("tmpo", [128, C], F32))
        psB = _st.enter_context(nc.psum_tensor("psB", [128, 512], F32))
        psC = _st.enter_context(nc.psum_tensor("psC", [128, 512], F32))
        psD = _st.enter_context(nc.psum_tensor("psD", [128, 1024], F32))
        psD2 = _st.enter_context(nc.psum_tensor("psD2", [128, 1024], F32))
        psE = _st.enter_context(nc.psum_tensor("psE", [128, 512], F32))
        psF = _st.enter_context(nc.psum_tensor("psF", [128, 512], F32))
        bank = lambda i: psD if i % 2 == 0 else psD2
        sem = {}
        for nm in (["T", "V", "S", "G", "dWG", "dWV", "dWO"]
                   + [f"dIN{i}" for i in range(4)]
                   + [f"dVO{i}" for i in range(2)]
                   + [f"dXI{i}" for i in range(4)]
                   + [f"dXO{i}" for i in range(4)]
                   + [f"dVG{i}" for i in range(4)]
                   + [f"dSC{i}" for i in range(2)]):
            sem[nm] = _st.enter_context(nc.semaphore("sem_" + nm))
        block = _st.enter_context(nc.Block())

        for dst, srch in ((triu, tri_h), (bvrow, bv_h),
                          (borow, bo_h)):
            S.op("gpsimd",
                 lambda e, d=dst, s=srch: e.dma_start(d[:, :], s[:, :]),
                 inc="dWG", amt=16)
        ev_consts_all = S.op("gpsimd",
                             lambda e: e.dma_start(ident[:, :], id_h[:, :]),
                             inc="dWG", amt=16)
        ev_ones = S.op("vector", lambda e: e.memset(onec[:, :], 1.0), inc="V")
        ev_onesb = S.op("vector", lambda e: e.memset(onecb[:, :], 1.0), inc="V")
        ev_oner = S.op("vector", lambda e: e.memset(oner[:, :], 1.0), inc="V")

        ev_bb = {}
        for bi, (row, tile) in enumerate(((bvrow, bvb), (borow, bob))):
            for h, ps in ((0, psB), (1, psC)):
                evm = S.op("tensor",
                           lambda e, r=row, h=h, ps=ps: e.matmul(
                               ps[:, 0:512], oner[0:1, :], r[0:1, h * 512:(h + 1) * 512],
                               start=True, stop=True),
                           waits=[ev_oner, ev_consts_all] + ([ev_bb.get((bi - 1, h))] if bi else []),
                           inc="T")
                ev_bb[(bi, h)] = S.op("vector",
                                      lambda e, t=tile, h=h, ps=ps: e.tensor_copy(
                                          t[:, h * 512:(h + 1) * 512], ps[:, 0:512]),
                                      waits=[evm], inc="V")

        LO = NCH * C
        stg_bufs = [(tmpo, 0), (xsp, 0), (dump, 0), (y32, 0)]
        stg_free = [None, None, None, None]
        last_dIN = {}

        def stage4(i, src_slice):
            buf, off = stg_bufs[i]
            eng, sm = (("sync", f"dIN{i}") if i < 2
                       else ("scalar", f"dSC{i - 2}"))
            evd = S.op(eng,
                       lambda e, s=src_slice, buf=buf, off=off: e.dma_start(
                           buf[:, off:off + C], s),
                       waits=[stg_free[i]], inc=sm, amt=16)
            if i < 2:
                last_dIN[i] = evd
            return evd

        ev_sign = [None] * NCH
        dma_ev = {}
        for ic in range(4):
            dma_ev[ic] = stage4(ic, w_t[ic * 128:(ic + 1) * 128, :])
        for ic in range(NCH):
            evd = dma_ev[ic]
            buf, off = stg_bufs[ic % 4]
            ev_sign[ic] = S.op("scalar",
                               lambda e, ic=ic, buf=buf, off=off: e.activation(
                                   swt[:, ic * C:(ic + 1) * C],
                                   buf[:, off:off + C], AF.Sign),
                               waits=[evd], inc="S")
            stg_free[ic % 4] = ev_sign[ic]
            if ic + 4 < NCH:
                dma_ev[ic + 4] = stage4(
                    ic, w_t[(ic + 4) * 128:(ic + 5) * 128, :])

        wbev = {}

        def emit_wb():
            evd_wb = S.op("sync",
                          lambda e: e.dma_start(
                              tmpo[:, 0:512].rearrange("p (ic r) -> p ic r",
                                                       ic=NCH),
                              wb_t.rearrange("(ic p) r -> p ic r", p=128)),
                          waits=[stg_free[0]], inc="dIN0", amt=16)
            last_dIN[0] = evd_wb
            wbev["h"] = S.op("scalar",
                             lambda e: e.activation(wbth[:, :], tmpo[:, 0:512],
                                                    AF.Copy),
                             waits=[evd_wb], inc="S")
            wbev["l"] = S.op("vector",
                             lambda e: e.tensor_sub(wbtl[:, :], tmpo[:, 0:512],
                                                    wbth[:, :]),
                             waits=[wbev["h"]], inc="V")
            stg_free[0] = wbev["l"]

        init_free = [None, None]
        ev_spill = {}
        wv_ready = [None]

        def emit_wchunk(j):
            if j == 0:
                init_free[0], init_free[1] = stg_free[0], stg_free[1]
            w_src, ic = (wo_t, j) if j < NCH else (wv_t, j - NCH)
            slot = j % 2
            buf, off = stg_bufs[slot]
            evd = S.op("sync",
                       lambda e, s=w_src[ic * 128:(ic + 1) * 128, :],
                       buf=buf, off=off: e.dma_start(
                           buf[:, off:off + C], s),
                       waits=[init_free[slot]], inc=f"dIN{slot}", amt=16)
            last_dIN[slot] = evd
            evh = S.op("scalar",
                       lambda e, ic=ic, buf=buf, off=off: e.activation(
                           warena[:, ic * C:(ic + 1) * C],
                           buf[:, off:off + C], AF.Copy),
                       waits=[evd, ev_spill.get("wo")], inc="S")
            evl = S.op("vector",
                       lambda e, ic=ic, buf=buf, off=off: e.tensor_sub(
                           warena[:, LO + ic * C:LO + (ic + 1) * C],
                           buf[:, off:off + C],
                           warena[:, ic * C:(ic + 1) * C]),
                       waits=[evh], inc="V")
            init_free[slot] = evl
            if j == NCH - 1:
                ev_spill["wo"] = S.op("sync",
                                      lambda e: e.dma_start(wol_scr[:, :],
                                                            warena[:, :]),
                                      waits=[evl], inc="dWV", amt=16)
            elif j == 2 * NCH - 1:
                wv_ready[0] = evl
                ev_spill["wv"] = S.op("sync",
                                      lambda e: e.dma_start(wvl_scr[:, :],
                                                            warena[:, :]),
                                      waits=[evl], inc="dWV", amt=16)

        ev_xout = {}
        ev_wvt = [None, None]
        ev_wot = None
        ev_outp_last = None
        carry_ffn2_hilo = {0: (ev_consts_all,), 1: (ev_consts_all,)}
        carry_accum = {(0, 0): None, (1, 0): None,
                       (0, 1): ev_bb[(1, 0)], (1, 1): ev_bb[(1, 1)]}
        carry_evrz15 = None
        carry_xin = {}
        carry_flush = None
        carry_pre_eva = {}
        x_hbm = x_in.rearrange("(k t p) c -> k p t c", p=128, t=4)
        y_hbm = y_out.rearrange("(k t p) c -> k p t c", p=128, t=4)

        def sb4(k):
            return x_sb[:, k * 4 * C:(k + 1) * 4 * C].rearrange(
                "p (t c) -> p t c", t=4)

        evsq_rd_done = [stg_free[2]]
        tr_ev = [None, None]

        def emit_stat_tile(po, ti, gates, src=None):
            sv = src if src is not None else x_sb[:, ti * C:(ti + 1) * C]
            evsum = S.op("vector",
                         lambda e, po=po, ti=ti, sv=sv: e.reduce_sum(
                             sums[:, po + ti:po + ti + 1], sv, axis=AX.X),
                         waits=gates, inc="V")
            evsq = S.op("scalar",
                        lambda e, ti=ti, sv=sv: e.activation(
                            dump[:, :], sv, AF.Square),
                        waits=list(gates) + [evsq_rd_done[0]], inc="S")
            evsq2 = S.op("vector",
                         lambda e, po=po, ti=ti: e.reduce_sum(
                             sqs[:, po + ti:po + ti + 1], dump[:, :], axis=AX.X),
                         waits=[evsq], inc="V")
            evsq_rd_done[0] = evsq2
            return [evsum, evsq2]

        def emit_finish(po, g, stat_evs):
            c0 = po + 4 * g
            sl = slice(c0, c0 + 4)
            evm0 = S.op("vector",
                        lambda e, sl=sl: e.tensor_scalar_mul(
                            mvec[:, sl], sums[:, sl], 1.0 / C),
                        waits=stat_evs, inc="V")
            evm1 = S.op("vector",
                        lambda e, sl=sl: e.tensor_mul(msq[:, sl], mvec[:, sl],
                                                      mvec[:, sl]),
                        waits=[evm0], inc="V")
            evv = S.op("vector",
                       lambda e, sl=sl: e.tensor_scalar(
                           varv[:, sl], sqs[:, sl], 1.0 / C, None, ALU.mult),
                       waits=stat_evs, inc="V")
            evv2 = S.op("vector",
                        lambda e, sl=sl: e.tensor_sub(varv[:, sl], varv[:, sl],
                                                      msq[:, sl]),
                        waits=[evm1, evv], inc="V")
            evv3 = S.op("vector",
                        lambda e, sl=sl: e.tensor_scalar_add(varv[:, sl],
                                                             varv[:, sl], EPS),
                        waits=[evv2], inc="V")
            evsq = S.op("scalar",
                        lambda e, sl=sl: e.activation(sqv[:, sl], varv[:, sl],
                                                      AF.Sqrt),
                        waits=[evv3], inc="S")
            evr = S.op("vector",
                       lambda e, sl=sl: e.reciprocal(rstd[:, sl], sqv[:, sl]),
                       waits=[evsq], inc="V")
            return evr

        ev_xin0 = {}
        for ti in range(NT):
            g, t = ti // 4, ti % 4
            ev_xin0[ti] = S.op("gpsimd",
                               lambda e, g=g, t=t: e.dma_start(
                                   x_sb[:, (4 * g + t) * C:
                                        (4 * g + t + 1) * C],
                                   x_hbm[g, :, t, :]),
                               waits=[ev_xin0.get(ti - 4)],
                               inc=f"dXI{ti % 4}", amt=16)
        finish_ffn1_b0 = {}
        b0_sevs = {}

        def emit_b0_tile(tj):
            g = tj // 4
            sevs = b0_sevs.setdefault(g, [])
            sevs += emit_stat_tile(0, tj, [ev_xin0[tj]])
            if tj % 4 == 3:
                finish_ffn1_b0[g] = emit_finish(0, g, sevs)

        for tj in range(8):
            emit_b0_tile(tj)
        carry_xin = dict(ev_xin0)

        for b in range(NB):
            finish_ffn1 = finish_ffn1_b0 if b == 0 else carry_finish_ffn1

            def ln_apply(ti, po, gate):
                sl = (ti % 2) * C
                war = [tr_ev[(ti - 2) % 2]] if tr_ev[(ti - 2) % 2] else []
                return S.op("vector",
                            lambda e, ti=ti, sl=sl, po=po: e.tensor_scalar(
                                xh32[:, sl:sl + C], x_sb[:, ti * C:(ti + 1) * C],
                                mvec[:, po + ti:po + ti + 1],
                                rstd[:, po + ti:po + ti + 1],
                                ALU.subtract, ALU.mult),
                            waits=[gate] + war,
                            inc="V")

            def ln_transpose(ti, eva, tr_free_evs):
                sl = (ti % 2) * C

                def fn(e, sl=sl, bk=bank(ti)):
                    last = None
                    for ic in range(NCH):
                        last = e.transpose(bk[:, ic * 128:(ic + 1) * 128],
                                           xh32[:, sl + ic * 128:sl + (ic + 1) * 128],
                                           ident[:, :])
                    return last
                evt = S.op("tensor", fn, waits=[eva] + tr_free_evs, inc="T")
                tr_ev[ti % 2] = evt
                return evt

            def run_ffn(sub, po, finish_evs, tr_carry, accum_carry,
                        tile_hook=None, inject_at2=None, pre_eva=None):
                evac_ev = dict(accum_carry)
                hilo_ev = {}
                mm_ev = {}
                trs = {}
                eva_d = {}

                def emit_eva(ti):
                    if ti >= NT:
                        return
                    if pre_eva and ti in pre_eva:
                        eva_d[ti] = pre_eva[ti]
                        return
                    eva_d[ti] = ln_apply(ti, po, finish_evs[ti // 4])

                def emit_tr(ti):
                    if ti >= NT:
                        return
                    free = ([hilo_ev[(ti - 2, "hi")], hilo_ev[(ti - 2, "lo")]]
                            if ti >= 2 else list(tr_carry.get(ti % 2, ())))
                    trs[ti] = ln_transpose(ti, eva_d[ti],
                                           [f for f in free if f])
                def emit_hilo(ti):
                    if ti >= NT:
                        return
                    evt = trs[ti]
                    sl = (ti % 2) * C
                    bk = bank(ti)
                    war = [mm_ev[(ti - 2, 1)]] if (ti - 2, 1) in mm_ev else []
                    ehi = S.op("scalar",
                               lambda e, sl=sl, bk=bk: e.activation(
                                   xhi[:, sl:sl + C], bk[:, :], AF.Copy),
                               waits=[evt] + war, inc="S")
                    elo = S.op("vector",
                               lambda e, sl=sl, bk=bk: e.tensor_sub(
                                   xlo[:, sl:sl + C], bk[:, :],
                                   xhi[:, sl:sl + C]),
                               waits=[evt, ehi] + war, inc="V")
                    hilo_ev[(ti, "hi")] = ehi
                    hilo_ev[(ti, "lo")] = elo
                emit_eva(0)
                emit_eva(1)
                emit_tr(0)
                emit_eva(2)
                emit_tr(1)
                emit_hilo(0)
                for ti in range(NT):
                    if ti == 6 and inject_at2 is not None:
                        inject_at2()
                    emit_eva(ti + 3)
                    emit_tr(ti + 2)
                    emit_hilo(ti + 1)
                    ehi = hilo_ev[(ti, "hi")]
                    elo = hilo_ev[(ti, "lo")]
                    sl = (ti % 2) * C
                    par = ti % 2
                    for h, ps in ((0, psE if par == 0 else psB),
                                  (1, psF if par == 0 else psC)):
                        def fn(e, sl=sl, h=h, ps=ps):
                            last = None
                            for j, src in enumerate((xhi, xlo)):
                                for ic in range(NCH):
                                    last = e.matmul(
                                        ps[:, 0:512],
                                        src[:, sl + ic * 128:sl + (ic + 1) * 128],
                                        swt[:, ic * C + h * 512:ic * C + (h + 1) * 512],
                                        start=(j == 0 and ic == 0),
                                        stop=(j == 1 and ic == NCH - 1))
                            return last
                        evm = S.op("tensor", fn,
                                   waits=[ehi, elo, evac_ev[(h, par)],
                                          ev_sign[7]], inc="T")
                        mm_ev[(ti, h)] = evm
                        evac_ev[(h, par)] = S.op(
                            "vector",
                            lambda e, ti=ti, h=h, ps=ps:
                            e.scalar_tensor_tensor(
                                x_sb[:, ti * C + h * 512:ti * C + (h + 1) * 512],
                                ps[:, 0:512], 0.5,
                                x_sb[:, ti * C + h * 512:ti * C + (h + 1) * 512],
                                ALU.mult, ALU.add),
                            waits=[evm], inc="V")
                        ffn_evac[(sub, ti, h)] = evac_ev[(h, par)]
                    if tile_hook is not None:
                        tile_hook(ti)
                return hilo_ev

            ffn_evac = {}

            po_f1 = (b % 2) * NT
            po_at = ((b + 1) % 2) * NT
            po_f2 = (b % 2) * NT
            finish_attn = {}

            at_stats = {}

            def attn_hook(ti):
                if b == 0:
                    if ti == 0:
                        emit_wb()
                    if ti < 8:
                        emit_b0_tile(8 + ti)
                sevs = at_stats.setdefault(ti // 4, [])
                sevs += emit_stat_tile(
                    po_at, ti, [ffn_evac[(0, ti, 0)], ffn_evac[(0, ti, 1)]])
                if ti % 4 == 3:
                    finish_attn[ti // 4] = emit_finish(po_at, ti // 4, sevs)
                if b == 0:
                    emit_wchunk(ti)

            hilo1 = run_ffn(0, po_f1, finish_ffn1,
                            tr_carry=carry_ffn2_hilo, accum_carry=carry_accum,
                            tile_hook=attn_hook, inject_at2=carry_flush,
                            pre_eva=carry_pre_eva)

            if b == 0:
                ev_wvt = [None, wv_ready[0]]
            else:
                ev_wvt = []
                for k in range(2):
                    ev_wvt.append(S.op("sync",
                                       lambda e, k=k: e.dma_start(
                                           warena[:, k * LO:(k + 1) * LO],
                                           wvl_scr[:, k * LO:(k + 1) * LO]),
                                       waits=[ev_outp_last, ev_spill["wv"]],
                                       inc="dWV", amt=16))

            lat_ev = None
            xfm_ev = {}
            vmm_done = {}
            vh_ev = {}
            vl_ev = {}
            vout_ev = {}
            atr = {}

            aeva_d = {}

            def emit_aeva(ti):
                if ti >= NT:
                    return
                aeva_d[ti] = ln_apply(ti, po_at, finish_attn[ti // 4])

            def emit_atr(ti):
                if ti >= NT:
                    return
                free = ([xfm_ev[ti - 2]] if ti >= 2 else
                        [hilo1.get((14 + ti, "hi")), hilo1.get((14 + ti, "lo"))])
                atr[ti] = ln_transpose(ti, aeva_d[ti], [f for f in free if f])
            evxh_d = {}

            def emit_bx(ti):
                if ti >= NT:
                    return
                evt = atr[ti]
                sl = (ti % 2) * C
                bk = bank(ti)
                free = [vmm_done[ti - 2]] if ti - 2 in vmm_done else []
                evxh = S.op("scalar",
                            lambda e, sl=sl, bk=bk: e.activation(
                                xhi[:, sl:sl + C], bk[:, :], AF.Copy),
                            waits=[evt] + free, inc="S")
                evxl = S.op("vector",
                            lambda e, sl=sl, bk=bk: e.tensor_sub(
                                xlo[:, sl:sl + C], bk[:, :],
                                xhi[:, sl:sl + C]),
                            waits=[evxh], inc="V")
                evxh_d[ti] = evxh
                xfm_ev[ti] = evxl
            emit_aeva(0)
            emit_aeva(1)
            emit_atr(0)
            emit_aeva(2)
            emit_atr(1)
            emit_bx(0)
            for ti in range(NT):
                emit_aeva(ti + 3)
                emit_atr(ti + 2)
                emit_bx(ti + 1)
                evxh = evxh_d[ti]
                evxl = xfm_ev[ti]
                sl = (ti % 2) * C

                def fn_lat(e, sl=sl, ti=ti):
                    last = None
                    for ic in range(NCH):
                        for j, (wl, src) in enumerate(
                                ((wbth, xhi), (wbtl, xhi), (wbth, xlo))):
                            last = e.matmul(
                                psF[0:64, 0:128],
                                wl[:, ic * R:(ic + 1) * R],
                                src[:, sl + ic * 128:sl + (ic + 1) * 128],
                                start=(ic == 0 and j == 0),
                                stop=(ic == NCH - 1 and j == 2))
                    return last
                evlm = S.op("tensor", fn_lat,
                            waits=[evxh, evxl, wbev["h"], wbev["l"],
                                   lat_ev if ti else
                                   ffn_evac[(0, NT - 2, 1)]], inc="T")
                lh_ev = S.op("scalar",
                             lambda e, ti=ti: e.activation(
                                 lath[0:64, ti * 128:(ti + 1) * 128],
                                 psF[0:64, 0:128], AF.Copy),
                             waits=[evlm], inc="S")
                lat_ev = S.op("vector",
                              lambda e, ti=ti: e.tensor_sub(
                                  latl[0:64, ti * 128:(ti + 1) * 128],
                                  psF[0:64, 0:128],
                                  lath[0:64, ti * 128:(ti + 1) * 128]),
                              waits=[lh_ev], inc="V")

                sl3 = (ti % 2) * 2 * C
                for h, ps in ((0, psB), (1, psC)):
                    def fn_v(e, sl=sl, h=h, ps=ps):
                        last = None
                        for ic in range(NCH):
                            for j, (src, woff) in enumerate(
                                    ((xhi, 0), (xhi, LO), (xlo, 0))):
                                last = e.matmul(
                                    ps[:, 0:512],
                                    src[:, sl + ic * 128:sl + (ic + 1) * 128],
                                    warena[:, woff + ic * C + h * 512:
                                           woff + ic * C + (h + 1) * 512],
                                    start=(ic == 0 and j == 0),
                                    stop=(ic == NCH - 1 and j == 2))
                        return last
                    evm = S.op("tensor", fn_v,
                               waits=[evxh, evxl, ev_wvt[1],
                                      vl_ev.get((ti - 1, h)),
                                      ffn_evac[(0, NT - 1, h)]],
                               inc="T")
                    vmm_done[ti] = evm
                    vh_ev[(ti, h)] = S.op(
                        "scalar",
                        lambda e, sl3=sl3, h=h, ps=ps: e.activation(
                            vwl[:, sl3 + h * 512:sl3 + (h + 1) * 512],
                            ps[:, 0:512], AF.Copy),
                        waits=[evm, vout_ev.get(ti - 2)], inc="S")
                    vl_ev[(ti, h)] = S.op(
                        "vector",
                        lambda e, sl3=sl3, h=h, ps=ps: e.tensor_sub(
                            vwl[:, sl3 + C + h * 512:sl3 + C + (h + 1) * 512],
                            ps[:, 0:512],
                            vwl[:, sl3 + h * 512:sl3 + (h + 1) * 512]),
                        waits=[vh_ev[(ti, h)]], inc="V")
                vout_ev[ti] = S.op("scalar",
                                   lambda e, ti=ti, sl3=sl3: e.dma_start(
                                       v_scr[ti * 128:(ti + 1) * 128, :],
                                       vwl[:, sl3:sl3 + 2 * C]),
                                   waits=[vl_ev[(ti, 0)], vl_ev[(ti, 1)]],
                                   inc=f"dVO{ti % 2}", amt=16)

            n2_banks = [psB, psC, psF, psD]
            n2_ev = []
            prev_evac = None
            for j in range(4):
                eva2 = S.op("vector",
                            lambda e, j=j: e.tensor_add(
                                dump[0:64, 512:1024],
                                lath[0:64, j * 512:(j + 1) * 512],
                                latl[0:64, j * 512:(j + 1) * 512]),
                            waits=[lat_ev, prev_evac, evsq_rd_done[0]],
                            inc="V")
                evq = S.op("scalar",
                           lambda e: e.activation(
                               dump[0:64, 512:1024], dump[0:64, 512:1024],
                               AF.Square),
                           waits=[eva2], inc="S")
                wait_bank = []
                if n2_banks[j] is psB:
                    wait_bank = [vl_ev[(NT - 1, 0)]]
                elif n2_banks[j] is psC:
                    wait_bank = [vl_ev[(NT - 1, 1)]]
                elif n2_banks[j] is psD:
                    wait_bank = [xfm_ev[NT - 2]]
                evm = S.op("tensor",
                           lambda e, j=j: e.matmul(
                               n2_banks[j][0:1, 0:512], onec[0:64, 0:1],
                               dump[0:64, 512:1024], start=True, stop=True),
                           waits=[evq, ev_ones] + wait_bank, inc="T")
                prev_evac = S.op("vector",
                                 lambda e, j=j: e.reduce_max(
                                     mx4[0:1, j:j + 1],
                                     n2_banks[j][0:1, 0:512], axis=AX.X),
                                 waits=[evm], inc="V")
                n2_ev.append(prev_evac)
            evmx = S.op("vector",
                        lambda e: e.reduce_max(mxrow[0:1, 0:1], mx4[0:1, :],
                                               axis=AX.X),
                        waits=n2_ev, inc="V")
            evcb_m = S.op("tensor",
                          lambda e: e.matmul(psE[:, 0:1], oner[0:1, :],
                                             mxrow[0:1, 0:1], start=True, stop=True),
                          waits=[evmx, ev_oner], inc="T")
            evcb = S.op("vector",
                        lambda e: e.tensor_scalar_mul(cbias[:, :], psE[:, 0:1],
                                                      -0.125),
                        waits=[evcb_m], inc="V")

            ev_wot = []
            for k in range(2):
                ev_wot.append(S.op("sync",
                                   lambda e, k=k: e.dma_start(
                                       warena[:, k * LO:(k + 1) * LO],
                                       wol_scr[:, k * LO:(k + 1) * LO]),
                                   waits=[vmm_done[NT - 1]] +
                                         ([ev_spill["wv"]] if b == 0 else []),
                                   inc="dWO", amt=16))

            pairs = [(qi, ki) for qi in range(NT) for ki in range(qi + 1)]
            vin_ev = {}
            ving_ev = {}
            emitted = [0]

            def emit_vin(n):
                while emitted[0] < min(n, len(pairs)):
                    idx = emitted[0]
                    qi, ki = pairs[idx]
                    slot = idx % 5
                    sm = f"dIN{slot}" if slot < 4 else "dVG0"
                    wait_slot = ([yz_ev[idx - 5]] if idx >= 5 else
                                 [last_dIN.get(slot)])
                    vin_ev[idx] = S.op(
                        "sync",
                        lambda e, ki=ki, slot=slot: e.dma_start(
                            vr[:, slot * 2 * C:(slot + 1) * 2 * C],
                            v_scr[ki * 128:(ki + 1) * 128, :]),
                        waits=wait_slot + [vout_ev[ki]],
                        inc=sm, amt=16)
                    emitted[0] += 1

            yz_ev = {}
            exp_ev = {}
            scr = {}
            yfl_ev = {}
            ytr_ev = {}
            evrz_ev = {}
            outp_evac = {0: None, 1: None}
            outp_holder = [None]
            oev_t = {}
            bob_pre = {}
            pend_tail = []
            finish_ffn2 = {}
            f2stats = {}
            emit_vin(5)

            def emit_f2_stats(ti):
                sevs = f2stats.setdefault(ti // 4, [])
                sevs += emit_stat_tile(po_f2, ti,
                                       [oev_t[(ti, 0)], oev_t[(ti, 1)]])
                if ti % 4 == 3:
                    finish_ffn2[ti // 4] = emit_finish(po_f2, ti // 4, sevs)

            def emit_score(idx):
                if idx >= len(pairs):
                    return
                qi, ki = pairs[idx]
                slot = idx % 3
                esl = 128 + slot * 128
                es = slot * 128
                def fn_st(e, qi=qi, ki=ki, esl=esl):
                    last = None
                    for j, (kl, ql) in enumerate(
                            ((lath, lath), (lath, latl), (latl, lath))):
                        last = e.matmul(
                            psE[:, esl:esl + 128],
                            kl[0:64, ki * 128:(ki + 1) * 128],
                            ql[0:64, qi * 128:(qi + 1) * 128],
                            start=(j == 0), stop=(j == 2))
                    return last
                evst = S.op("tensor", fn_st,
                            waits=[lat_ev, evcb, exp_ev.get(idx - 1)], inc="T")
                eve = S.op("scalar",
                           lambda e, esl=esl, es=es: e.activation(
                               e_sb[:, es:es + 128], psE[:, esl:esl + 128],
                               AF.Exp, bias=cbias[:, 0:1], scale=0.125),
                           waits=[evst, evcb, yz_ev.get(idx - 3)], inc="S")
                exp_ev[idx] = eve
                gate = eve
                if ki == qi:
                    gate = S.op("vector",
                                lambda e, es=es: e.tensor_mul(
                                    e_sb[:, es:es + 128], e_sb[:, es:es + 128],
                                    triu[:, :]),
                                waits=[eve, ev_consts_all], inc="V")
                evh = S.op("vector",
                           lambda e, es=es: e.tensor_copy(
                               ehb[:, es:es + 128], e_sb[:, es:es + 128]),
                           waits=[gate], inc="V")
                evl = S.op("vector",
                           lambda e, es=es: e.tensor_sub(
                               elb[:, es:es + 128], e_sb[:, es:es + 128],
                               ehb[:, es:es + 128]),
                           waits=[evh], inc="V")
                scr[idx] = (gate, evh, evl)

            def emit_rz(qi):
                idx_d = qi * (qi + 1) // 2 + qi
                zc = qi % 2
                evrz_ev[qi] = S.op("vector",
                                   lambda e, zc=zc: e.reciprocal(
                                       rz[:, zc:zc + 1], psF[:, 0:1]),
                                   waits=[yz_ev[idx_d]], inc="V")

            def emit_tail_a(qi):
                pd = bank(qi)
                zc = qi % 2
                y32_ev = S.op("vector",
                              lambda e, pd=pd, zc=zc: e.scalar_tensor_tensor(
                                  y32[:, :], pd[:, :], rz[:, zc:zc + 1],
                                  bvb[:, :], ALU.mult, ALU.add),
                              waits=[evrz_ev[qi], ev_bb[(0, 0)], ev_bb[(0, 1)],
                                     ytr_ev.get(qi - 1)],
                              inc="V")

                def fn_ytr(e, pd=pd):
                    last = None
                    for cc in range(NCH):
                        last = e.transpose(pd[:, cc * 128:(cc + 1) * 128],
                                           y32[:, cc * 128:(cc + 1) * 128],
                                           ident[:, :])
                    return last
                ytr_ev[qi] = S.op("tensor", fn_ytr, waits=[y32_ev], inc="T")

            def emit_tail_b(qi):
                pd = bank(qi)
                evyt = ytr_ev[qi]
                yfh = S.op("scalar",
                           lambda e, pd=pd: e.activation(yfml[:, 0:C],
                                                         pd[:, :], AF.Copy),
                           waits=[evyt], inc="S")
                yfl = S.op("vector",
                           lambda e, pd=pd: e.tensor_sub(
                               yfml[:, C:2 * C], pd[:, :], yfml[:, 0:C]),
                           waits=[yfh], inc="V")
                yfl_ev[qi] = yfl
                bob_pre[qi] = S.op("vector",
                                   lambda e, qi=qi: e.tensor_add(
                                       x_sb[:, qi * C:(qi + 1) * C],
                                       x_sb[:, qi * C:(qi + 1) * C],
                                       bob[:, :]),
                                   waits=[aeva_d[qi], ev_bb[(1, 0)],
                                          ev_bb[(1, 1)]],
                                   inc="V")
                for h, ps in ((0, psB), (1, psC)):
                    def fn_op(e, h=h, ps=ps):
                        last = None
                        for cc in range(NCH):
                            for j, (aoff, woff) in enumerate(
                                    ((0, 0), (0, LO), (C, 0))):
                                last = e.matmul(
                                    ps[:, 0:512],
                                    yfml[:, aoff + cc * 128:
                                         aoff + (cc + 1) * 128],
                                    warena[:, woff + cc * C + h * 512:
                                           woff + cc * C + (h + 1) * 512],
                                    start=(cc == 0 and j == 0),
                                    stop=(cc == NCH - 1 and j == 2))
                        return last
                    wb = [yfh, yfl, ev_wot[1], outp_evac[h]]
                    if qi == 0:
                        wb.append(n2_ev[0 if h == 0 else 1])
                    evm = S.op("tensor", fn_op, waits=wb, inc="T")
                    outp_holder[0] = evm
                    evv = S.op("vector",
                               lambda e, qi=qi, h=h, ps=ps: e.tensor_add(
                                   x_sb[:, qi * C + h * 512:
                                        qi * C + (h + 1) * 512],
                                   ps[:, 0:512],
                                   x_sb[:, qi * C + h * 512:
                                        qi * C + (h + 1) * 512]),
                               waits=[evm, bob_pre[qi]], inc="V")
                    outp_evac[h] = evv
                    oev_t[(qi, h)] = evv
                if qi >= 1:
                    emit_f2_stats(qi - 1)

            emit_score(0)
            emit_score(1)
            pre_eva2 = {}
            for idx, (qi, ki) in enumerate(pairs):
                emit_vin(idx + 5)
                while pend_tail and pend_tail[0][0] <= idx:
                    pend_tail.pop(0)[1]()
                if idx == 104:
                    pre_eva2[0] = ln_apply(0, po_f2, finish_ffn2[0])
                    pre_eva2[1] = ln_apply(1, po_f2, finish_ffn2[0])
                emit_score(idx + 2)
                gate, evh, evl = scr[idx]
                es = (idx % 3) * 128
                vslot = idx % 5
                pd = bank(qi)

                def fn_yz(e, qi=qi, ki=ki, vslot=vslot, es=es, pd=pd):
                    st = ki == 0
                    sp = ki == qi
                    base = vslot * 2 * C
                    for j, (ab, voff) in enumerate(((ehb, 0), (ehb, C), (elb, 0))):
                        e.matmul(pd[:, 0:512], ab[:, es:es + 128],
                                 vr[:, base + voff:base + voff + 512],
                                 start=(st and j == 0), stop=(sp and j == 2))
                        e.matmul(pd[:, 512:1024], ab[:, es:es + 128],
                                 vr[:, base + voff + 512:base + voff + 1024],
                                 start=(st and j == 0), stop=(sp and j == 2))
                    return e.matmul(psF[:, 0:1],
                                    e_sb[:, es:es + 128],
                                    onec[:, 0:1], start=st, stop=sp)
                w = [gate, evh, evl, vin_ev[idx]]
                if ki == 0:
                    if qi >= 1:
                        w.append(evrz_ev[qi - 1])
                    if qi >= 2:
                        w.append(yfl_ev[qi - 2])
                    elif qi < 2:
                        w.append(xfm_ev[NT - 2 + qi])
                yz_ev[idx] = S.op("tensor", fn_yz, waits=w, inc="T")
                if ki == qi:
                    emit_rz(qi)
                    pend_tail.append((idx + 2, lambda qi=qi: emit_tail_a(qi)))
                    pend_tail.append((idx + 3, lambda qi=qi: emit_tail_b(qi)))
            for _, fn in pend_tail:
                fn()
            ev_outp_last = outp_holder[0]
            carry_evrz15 = evrz_ev[NT - 1]

            carry_finish_ffn1 = {}
            po_nx = ((b + 1) % 2) * NT
            xin_tile = {}
            nx_stats = {}
            xsp_cp = [None]
            pre_eva_nx = {}

            def emit_nx_stats(ti):
                sevs = nx_stats.setdefault(ti // 4, [])
                sevs += emit_stat_tile(
                    po_nx, ti, [xin_tile[ti]],
                    src=(xsp[:, :] if ti == NT - 1 else None))
                if ti == NT - 1:
                    sevs.append(xsp_cp[0])
                if ti % 4 == 3:
                    carry_finish_ffn1[ti // 4] = emit_finish(
                        po_nx, ti // 4, sevs)

            def store_hook(ti):
                if ti == 0:
                    emit_f2_stats(NT - 1)
                g, t = ti // 4, ti % 4
                ev_xout[(b, ti)] = S.op(
                    "gpsimd",
                    lambda e, b=b, g=g, t=t, ti=ti: e.dma_start(
                        y_hbm[b * 4 + g, :, t, :],
                        x_sb[:, ti * C:(ti + 1) * C]),
                    waits=[ffn_evac[(2, ti, 0)], ffn_evac[(2, ti, 1)]],
                    inc=f"dXO{ti % 4}", amt=16)
                if b + 1 < NB:
                    if ti == NT - 1:
                        xin_tile[ti] = S.op(
                            "gpsimd",
                            lambda e, b=b: e.dma_start(
                                xsp[:, :], x_hbm[(b + 1) * 4 + 3, :, 3, :]),
                            waits=[xin_tile.get(ti - 4,
                                                carry_xin.get(ti + 12)),
                                   finish_ffn1[3]] +
                                  ([init_free[1]] if b == 0 else []),
                            inc=f"dXI{ti % 4}", amt=16)
                        xsp_cp[0] = S.op(
                            "gpsimd",
                            lambda e, ti=ti: e.tensor_copy(
                                x_sb[:, ti * C:(ti + 1) * C], xsp[:, :]),
                            waits=[ev_xout[(b, ti)], xin_tile[ti]], inc="G")
                    else:
                        xin_tile[ti] = S.op(
                            "gpsimd",
                            lambda e, b=b, g=g, t=t, ti=ti: e.dma_start(
                                x_sb[:, ti * C:(ti + 1) * C],
                                x_hbm[(b + 1) * 4 + g, :, t, :]),
                            waits=[ev_xout[(b, ti)],
                                   xin_tile.get(ti - 4,
                                                carry_xin.get(ti + 12))],
                            inc=f"dXI{ti % 4}", amt=16)
                    if ti >= 1:
                        emit_nx_stats(ti - 1)
                    if ti == 14:
                        pre_eva_nx[0] = ln_apply(0, po_nx,
                                                 carry_finish_ffn1[0])
                    elif ti == 15:
                        pre_eva_nx[1] = ln_apply(1, po_nx,
                                                 carry_finish_ffn1[0])

            hilo2 = run_ffn(2, po_f2, finish_ffn2,
                            tr_carry={0: (yfl_ev[NT - 2],),
                                      1: (yfl_ev[NT - 1],)},
                            accum_carry={(0, 0): exp_ev[len(pairs) - 1],
                                         (1, 0): evrz_ev[NT - 1],
                                         (0, 1): outp_evac[0],
                                         (1, 1): outp_evac[1]},
                            tile_hook=store_hook, pre_eva=pre_eva2)
            if b + 1 < NB:
                emit_nx_stats(NT - 1)
            carry_ffn2_hilo = {0: (hilo2[(NT - 2, "hi")], hilo2[(NT - 2, "lo")]),
                               1: (hilo2[(NT - 1, "hi")], hilo2[(NT - 1, "lo")])}
            carry_accum = {(0, 0): ffn_evac[(2, NT - 2, 0)],
                           (1, 0): ffn_evac[(2, NT - 2, 1)],
                           (0, 1): ffn_evac[(2, NT - 1, 0)],
                           (1, 1): ffn_evac[(2, NT - 1, 1)]}
            carry_xin = dict(xin_tile)
            carry_pre_eva = pre_eva_nx

        finals = [Ev(f"dXO{k}", S.cnt[f"dXO{k}"], None) for k in range(4)]
        S.op("sync", None, waits=finals)

        def mk_body(name):
            def body(eng):
                hi = {}
                for (e, waits, fn, inc, amt) in S.ops:
                    if e != name:
                        continue
                    for ev in waits:
                        if hi.get(ev.sem, 0) >= ev.val:
                            continue
                        eng.wait_ge(sem[ev.sem], ev.val)
                        hi[ev.sem] = ev.val
                    if fn is not None:
                        ins = fn(eng)
                        if inc is not None:
                            ins.then_inc(sem[inc], amt)
            return body

        block.tensor(mk_body("tensor"))
        block.vector(mk_body("vector"))
        block.scalar(mk_body("scalar"))
        block.gpsimd(mk_body("gpsimd"))
        block.sync(mk_body("sync"))

    nc._sched = S
    return nc


def kernel(L0_attn_hidden_states, L0_attn_W_basis, L0_attn_W_v, L0_attn_b_v,
           L0_attn_W_o, L0_attn_b_o, L0_ffn_weight):
    x = np.ascontiguousarray(np.asarray(L0_attn_hidden_states, dtype=np.float32))
    wv_t = np.ascontiguousarray(np.asarray(L0_attn_W_v, np.float32).T)
    wo_t = np.ascontiguousarray(np.asarray(L0_attn_W_o, np.float32).T)
    wb_t = np.ascontiguousarray(np.asarray(L0_attn_W_basis, np.float32).T)
    w_t = np.ascontiguousarray(np.asarray(L0_ffn_weight, np.float32).T)
    bv = np.ascontiguousarray(np.asarray(L0_attn_b_v, np.float32).reshape(1, C))
    bo = np.ascontiguousarray(np.asarray(L0_attn_b_o, np.float32).reshape(1, C))
    ident = np.eye(128, dtype=np.float32)
    tri = np.triu(np.ones((128, 128), dtype=np.float32))

    nc = build_program()
    in_maps = []
    for c in range(8):
        xc = np.ascontiguousarray(x[c * NB:(c + 1) * NB].reshape(NB * T, C))
        in_maps.append({"x_in": xc, "wv_t": wv_t, "wo_t": wo_t, "wb_t": wb_t,
                        "w_t": w_t, "bv": bv, "bo": bo, "ident": ident,
                        "triu": tri})
    res = run_bass_kernel_spmd(nc, in_maps, core_ids=list(range(8)))
    out = np.concatenate([np.asarray(r["y_out"]).reshape(NB, T, C)
                          for r in res.results], axis=0)
    return out.astype(np.float32, copy=False)
